# version 23
# baseline (speedup 1.0000x reference)
import numpy as np
import ml_dtypes
import concourse.bass as bass
import concourse.mybir as mybir
from concourse.bass_utils import run_bass_kernel_spmd

import concourse.bass_utils as _bu

if not getattr(_bu, "_ldwopt_patched", False):
    _orig_run_command = _bu.run_command

    def _patched_run_command(argv, **kwargs):
        argv = [a.replace("--enable-ldw-opt=false", "--enable-ldw-opt=true")
                if isinstance(a, str) else a for a in argv]
        return _orig_run_command(argv, **kwargs)

    _bu.run_command = _patched_run_command
    _bu._ldwopt_patched = True

F32 = mybir.dt.float32
BF16 = mybir.dt.bfloat16
AF = mybir.ActivationFunctionType
ALU = mybir.AluOpType
AX = mybir.AxisListType

NCORE = 8
T = 2048
C = 1024
FC = 8
NP = 4
NT = 16
H = 16
HD = 64
HC = 8
LMG = 64
LML = 16
E = 8
NJ = 16
NTB = 2

PHASES = ("ffn1", "attn", "ffn2")


class Prog:
    ENGINES = ("pe", "act", "dve", "pool", "sp")

    def __init__(self):
        self.ops = []

    def add(self, engine, fn, r=(), w=(), kind="c", tag=""):
        self.ops.append(dict(e=engine, fn=fn, r=tuple(r), w=tuple(w),
                             kind=kind, tag=tag, marked=False, token=None))
        return len(self.ops) - 1

    def dma(self, queue, out, in_, r=(), w=(), tag=""):
        def fn(eng, out=out, in_=in_):
            return eng.dma_start(out=out, in_=in_)
        return self.add(queue, fn, r=r, w=w, kind="d", tag=tag)

    def fence(self, key, extra_w=(), tag="fence"):
        return self.add("sp", lambda eng: eng.nop(), r=(),
                        w=(key,) + tuple(extra_w), kind="c", tag=tag)

    def finalize_and_emit(self, block, sems):
        ops = self.ops
        last_writer, readers = {}, {}
        deps_all = []
        for i, op in enumerate(ops):
            deps = set()
            for k in op["r"]:
                if k in last_writer:
                    deps.add(last_writer[k])
            for k in op["w"]:
                if k in last_writer:
                    deps.add(last_writer[k])
                for j in readers.get(k, ()):
                    deps.add(j)
            deps.discard(i)
            deps_all.append(deps)
            for k in op["r"]:
                readers.setdefault(k, []).append(i)
            for k in op["w"]:
                last_writer[k] = i
                readers[k] = []
        for i, op in enumerate(ops):
            for d in deps_all[i]:
                if ops[d]["e"] == "pe" and op["e"] == "pe" \
                        and ops[d]["kind"] == "c":
                    continue
                ops[d]["marked"] = True
        ccount = {e: 0 for e in self.ENGINES}
        dcount = {"sp": 0, "pool": 0}
        nccc = 0
        for op in ops:
            if op["kind"] == "d":
                k = dcount[op["e"]]
                dcount[op["e"]] += 1
                op["token"] = (f"{op['e']}_d{k % 4}", 16 * (k // 4 + 1))
                op["dpred"] = (f"{op['e']}_d{k % 4}", 16 * (k // 4)) \
                    if k >= 4 else None
            elif op["kind"] == "cc":
                op["token"] = (f"cc{nccc}", 1)
                nccc += 1
            elif op["marked"]:
                ccount[op["e"]] += 1
                op["token"] = (op["e"], ccount[op["e"]])
        self.final_counts = {e: ccount[e] for e in self.ENGINES}
        for q in ("sp", "pool"):
            for i4 in range(4):
                n = dcount[q] // 4 + (1 if dcount[q] % 4 > i4 else 0)
                self.final_counts[f"{q}_d{i4}"] = 16 * n
        for i in range(nccc):
            self.final_counts[f"cc{i}"] = 1
        per_engine = {e: [] for e in self.ENGINES}
        for i, op in enumerate(ops):
            per_engine[op["e"]].append(i)

        def emit_for(engine, eng):
            last_waited = {}
            for i in per_engine[engine]:
                op = ops[i]
                waits = {}
                for d in deps_all[i]:
                    dop = ops[d]
                    if dop["kind"] == "c" and dop["e"] == "pe" \
                            and engine == "pe":
                        continue
                    tok = dop["token"]
                    assert tok is not None, (ops[d]["tag"], op["tag"])
                    s, v = tok
                    waits[s] = max(waits.get(s, 0), v)
                for s, v in sorted(waits.items()):
                    if last_waited.get(s, 0) < v:
                        eng.wait_ge(sems[s], v)
                        last_waited[s] = v
                if op["kind"] == "d" and op.get("dpred"):
                    ss, vv = op["dpred"]
                    if last_waited.get(ss, 0) < vv:
                        eng.wait_ge(sems[ss], vv)
                        last_waited[ss] = vv
                inst = op["fn"](eng)
                if op["kind"] == "d":
                    inst.then_inc(sems[op["token"][0]], 16)
                elif op["kind"] == "cc":
                    inst.then_inc(sems[op["token"][0]], 1)
                elif op["marked"]:
                    inst.then_inc(sems[op["e"]], 1)

        @block.tensor
        def _(eng):
            emit_for("pe", eng)

        @block.scalar
        def _(eng):
            emit_for("act", eng)

        @block.vector
        def _(eng):
            emit_for("dve", eng)

        @block.gpsimd
        def _(eng):
            emit_for("pool", eng)

        fc_ = self.final_counts

        @block.sync
        def _(eng):
            emit_for("sp", eng)
            for s, v in sorted(fc_.items()):
                if v and s in sems:
                    eng.wait_ge(sems[s], v)


def build_program(phases=PHASES):
    nc = bass.Bass("TRN2", target_bir_lowering=False, debug=False)
    p = Prog()

    x_h = nc.dram_tensor("x", [T, C], F32, kind="ExternalInput")
    wqkv_h = nc.dram_tensor("w_qkv", [C, 3 * C], F32, kind="ExternalInput")
    wproj_h = nc.dram_tensor("w_proj", [C, C], F32, kind="ExternalInput")
    W1_h = nc.dram_tensor("W1", [E, C, 4096], F32, kind="ExternalInput")
    b1_h = nc.dram_tensor("b1", [E, 4096], F32, kind="ExternalInput")
    W2_h = nc.dram_tensor("W2", [E, 2048, C], F32, kind="ExternalInput")
    b2_h = nc.dram_tensor("b2", [E, C], F32, kind="ExternalInput")
    Wr_h = nc.dram_tensor("Wr", [C, E], F32, kind="ExternalInput")
    br_h = nc.dram_tensor("br", [1, E], F32, kind="ExternalInput")
    eyef_h = nc.dram_tensor("eye_f", [128, 128], F32, kind="ExternalInput")
    eyes_h = nc.dram_tensor("eye64s", [128, 64], F32, kind="ExternalInput")
    esel_h = nc.dram_tensor("esel", [E, 1024], BF16, kind="ExternalInput")
    y_h = nc.dram_tensor("y", [T, C], F32, kind="ExternalOutput")

    cc1_in = nc.dram_tensor("cc1_in", [LML, C], BF16, kind="Internal")
    cc1_out = nc.dram_tensor("cc1_out", [LMG, C], BF16, kind="Internal")
    cc2_in = [nc.dram_tensor(f"cc2_in{i}", [64, 520], F32, kind="Internal")
              for i in range(2)]
    cc2_out = [nc.dram_tensor(f"cc2_out{i}", [256, 520], F32, kind="Internal")
               for i in range(2)]

    from contextlib import ExitStack
    with ExitStack() as ctx:
        def sb(name, shape, dtype):
            return ctx.enter_context(nc.sbuf_tensor(name, shape, dtype))

        slot_arena = nc.alloc_sbuf_tensor("slot_arena", [128, 16384], BF16)
        sa_addr = nc.lookup_mloc(slot_arena).addr
        slots = [nc.alloc_sbuf_tensor_at(f"slot{i}", [128, 8, 512], BF16,
                                         offset=sa_addr + 8192 * i)
                 for i in range(4)]
        stg = nc.alloc_sbuf_tensor_at("stg", [128, 4, 1024], F32,
                                      offset=sa_addr + 16384)
        cpay = nc.alloc_sbuf_tensor_at("cpay", [64, 1040], F32,
                                       offset=sa_addr + 16384)
        svz = nc.alloc_sbuf_tensor_at("svz", [64, 1040], F32,
                                      offset=sa_addr + 24576)
        m1_arena = nc.alloc_sbuf_tensor("m1_arena", [128, 16384], BF16)
        m1_addr = nc.lookup_mloc(m1_arena).addr
        E1 = nc.alloc_sbuf_tensor_at("E1", [128, HC, T], BF16, offset=m1_addr)
        actb = nc.alloc_sbuf_tensor_at("actb", [128, NJ, 1024], BF16,
                                       offset=m1_addr)
        LMf = nc.alloc_sbuf_tensor_at("LMf", [64, 1024], F32,
                                      offset=m1_addr + 7 * 4096)
        attn_s = sb("attn_s", [128, 2048], BF16)

        xF = sb("xF", [128, FC, T], F32)
        xnsv = sb("xnsv", [128, FC, T], BF16)
        poolf = [sb(f"pf{i}", [128, 512], F32) for i in range(6)]
        poolb = [sb(f"pb{i}", [128, 512], BF16) for i in range(4)]
        e3t = sb("e3t", [128, 8, HD], BF16)
        b1raw = sb("b1raw", [E, 256], F32)
        rowp = sb("rowp", [128, 512], F32)
        rowb = sb("rowb", [1, 1024], BF16)
        invs2 = [sb(f"invs{i}", [128, 8, 64], F32) for i in range(2)]
        invT = sb("invT", [128, HC, 64], BF16)
        Gsb = sb("Gsb", [128, HC, 64], BF16)
        qlmT = sb("qlmT", [128, HC, LMG], BF16)
        klmT = sb("klmT", [128, HC, LMG], BF16)
        xlmF = sb("xlmF", [128, FC, LMG], BF16)
        xlm = sb("xlm", [128, FC, LML], F32)
        b2sb = sb("b2sb", [E, C], BF16)
        wrsb = sb("wrsb", [128, FC, E], BF16)
        brbc = sb("brbc", [128, E], F32)
        brraw = sb("brraw", [1, E], F32)
        brbf = sb("brbf", [1, E], BF16)
        wbT = sb("wbT", [E, T], BF16)
        eyef = sb("eyef", [128, 128], F32)
        eyeS = sb("eyeS", [128, 64], F32)
        onesf = sb("onesf", [128, 64], F32)
        onesb = sb("onesb", [128, 128], BF16)
        esel = sb("esel_sb", [E, 1024], BF16)
        lgt = sb("lgt", [128, E], F32)
        mxt = sb("mxt", [128, E], F32)
        e1w = sb("e1w", [128, E], F32)
        e2w = sb("e2w", [128, E], F32)
        s1c = sb("s1c", [128, 4], F32)
        wtt = sb("wtt", [128, E], F32)
        rzh = sb("rzh", [64, H], F32)
        epsb = sb("epsb", [128, 1], F32)
        sv3n = sb("sv3n", [128, 64], BF16)
        xlmT = attn_s[0:LML, 0:1024]

        ps = {i: ctx.enter_context(nc.psum_tensor(f"pbank{i}", [128, 512], F32))
              for i in range(8)}

        sems = {}
        for name in ("pe", "act", "dve", "pool", "sp",
                     "sp_d0", "sp_d1", "sp_d2", "sp_d3",
                     "pool_d0", "pool_d1", "pool_d2", "pool_d3",
                     "cc0", "cc1", "cc2", "cc3"):
            sems[name] = ctx.enter_context(nc.semaphore(name + "_s"))
        block = ctx.enter_context(nc.Block())

        def mm(bank, lhsT, rhs, start, stop, rkeys, tag, outsl=None,
               extra_w=(), tile=None, skipg=False):
            out = ps[bank][:, :] if outsl is None else outsl

            def fn(eng, out=out, lhsT=lhsT, rhs=rhs, start=start, stop=stop,
                   tile=tile, skipg=skipg):
                return eng.matmul(out, lhsT, rhs, start=start, stop=stop,
                                  tile_position=tile, skip_group_check=skipg)
            r = tuple(rkeys) + (() if start else (f"ps{bank}",))
            return p.add("pe", fn, r=r, w=(f"ps{bank}",) + tuple(extra_w),
                         tag=tag)

        def transpose(bank, in_, ident, rkeys, tag, outsl):
            def fn(eng, outsl=outsl, in_=in_, ident=ident):
                return eng.transpose(outsl, in_, ident)
            return p.add("pe", fn, r=tuple(rkeys), w=(f"ps{bank}",), tag=tag)

        p.dma("sp", eyef[:, :], eyef_h[:, :], w=("eyef",), tag="ld_eye")
        p.dma("sp", eyeS[:, :], eyes_h[:, :], w=("eyeS",), tag="ld_eyes")
        for hp_ in range(8):
            hsel = hp_ % 2
            hkey = f"stgh{hsel}"
            p.dma("sp", stg[:, 2 * hsel:2 * hsel + 2, :],
                  x_h[256 * hp_:256 * (hp_ + 1), :].rearrange(
                      "(tp q) f -> q tp f", q=128),
                  r=(), w=(hkey, f"slot{2 + hsel}"), tag=f"ld_x{hp_}")
            pp = hp_ // 2
            for tt in range(2):
                t = 2 * hp_ + tt
                for fc in range(FC):
                    bk = 6 if (tt * FC + fc) % 2 == 0 else 7
                    mm(bk, stg[:, 2 * hsel + tt, 128 * fc:128 * (fc + 1)],
                       eyef[:, :], True, True,
                       (hkey, f"slot{2 + hsel}", "eyef"),
                       f"tr_x{t}_{fc}", outsl=ps[bk][:, 0:128])

                    def fev(eng, fc=fc, t=t, bk=bk):
                        return eng.copy(xF[:, fc, 128 * t:128 * (t + 1)],
                                        ps[bk][:, 0:128])
                    p.add("act", fev, r=(f"ps{bk}", hkey),
                          w=(f"xF{fc}_{pp}",), tag=f"ev_x{t}_{fc}")

        p.dma("sp", esel[:, :], esel_h[:, :], w=("esel",), tag="ld_esel")
        p.add("pool", lambda e: e.memset(onesf[:, :], 1.0), w=("onesf",),
              tag="ones_f")
        p.add("pool", lambda e: e.memset(onesb[:, :], 1.0), w=("onesb",),
              tag="ones_b")
        p.add("pool", lambda e: e.memset(epsb[:, :], 1e-5), w=("epsb",),
              tag="eps_b")
        p.dma("pool", b2sb[:, :], b2_h[:, :], w=("b2sb",), tag="ld_b2")
        p.dma("pool", wrsb[:, :, :],
              Wr_h.ap().rearrange("(kc q) e -> q kc e", q=128),
              w=("wrsb",), tag="ld_wr")
        p.dma("sp", brraw[:, :], br_h[:, :], w=("brraw",), tag="ld_br")
        b1T = sb("b1T", [128, E, 32], F32)
        for piece in range(16):
            p.dma("sp", b1raw[:, :], b1_h[:, 256 * piece:256 * (piece + 1)],
                  r=("b1raw",), w=("b1raw",), tag=f"ld_b1_{piece}")
            for jj in range(2):
                j = 2 * piece + jj
                bk = 6 if j % 2 == 0 else 7
                transpose(bk, b1raw[0:E, 128 * jj:128 * (jj + 1)],
                          eyef[0:E, 0:E], ("b1raw", "eyef"), f"tr_b1_{j}",
                          outsl=ps[bk][:, 0:E])

                def fev(eng, j=j, bk=bk):
                    return eng.tensor_copy(b1T[:, :, j], ps[bk][:, 0:E])
                p.add("dve", fev, r=(f"ps{bk}", "b1raw"), w=(f"b1T{j}",),
                      tag=f"ev_b1_{j}")
        p.add("dve", lambda e: e.tensor_copy(brbf[:, :], brraw[:, :]),
              r=("brraw",), w=("brbf",), tag="br_bf")
        mm(6, onesb[0:1, :], brbf[:, :], True, True, ("onesb", "brbf"),
           "br_bc_mm", outsl=ps[6][:, 0:E])
        p.add("act", lambda e: e.copy(brbc[:, :], ps[6][:, 0:E]),
              r=("ps6",), w=("brbc",), tag="br_bc_ev")

        def layer_norm(name, pieces=tuple(range(NP))):
            mr = rowp[0:1, :]
            vv = rowp[32:33, :]
            lv = rowp[64:65, :]
            mbf = rowb[:, 0:512]
            rbf = rowb[:, 512:1024]

            def stats(pp):
                sA, sB = (0, 1) if pp % 2 == 0 else (4, 5)
                for fc in range(FC):
                    xq = poolb[2 + fc % 2]
                    mm(sA, onesf[:, 0:1], xF[:, fc, 512 * pp:512 * (pp + 1)],
                       fc == 0, fc == FC - 1, (f"xF{fc}_{pp}", "onesf"),
                       f"{name}_sum{pp}_{fc}", outsl=ps[sA][0:1, :])

                    def fsq(eng, fc=fc, pp=pp, xq=xq):
                        return eng.activation(
                            xq[:, :], xF[:, fc, 512 * pp:512 * (pp + 1)],
                            AF.Square)
                    p.add("act", fsq, r=(f"xF{fc}_{pp}",),
                          w=(f"pb{2 + fc % 2}",), tag=f"{name}_sq{pp}_{fc}")
                    mm(sB, onesb[:, 0:1], xq[:, :], fc == 0, fc == FC - 1,
                       (f"pb{2 + fc % 2}", "onesb"), f"{name}_ssq{pp}_{fc}",
                       outsl=ps[sB][0:1, :])

            def finish(pp):
                sA, sB = (0, 1) if pp % 2 == 0 else (4, 5)
                sC, sD = (2, 3) if pp % 2 == 0 else (6, 7)
                p.add("dve", lambda e, sA=sA: e.tensor_scalar(
                    mr, ps[sA][0:1, :], 1.0 / C, None, ALU.mult),
                    r=(f"ps{sA}",), w=("rs0",), tag=f"{name}_mean{pp}")
                p.add("dve", lambda e: e.tensor_copy(mbf, mr),
                      r=("rs0",), w=("rowmb",), tag=f"{name}_mbf{pp}")
                p.add("dve", lambda e: e.tensor_tensor(vv, mr, mr, ALU.mult),
                      r=("rs0",), w=("rs32",), tag=f"{name}_msq{pp}")
                p.add("dve", lambda e, sB=sB: e.scalar_tensor_tensor(
                    vv, ps[sB][0:1, :], 1.0 / C, vv, ALU.mult, ALU.subtract),
                    r=(f"ps{sB}", "rs32"), w=("rs32",), tag=f"{name}_var{pp}")
                p.add("act", lambda e: e.activation(lv, vv, AF.Ln,
                                                    bias=epsb[0:1, 0:1]),
                      r=("rs32", "epsb"), w=("rs64",), tag=f"{name}_lnv{pp}")
                p.add("act", lambda e: e.activation(mr, lv, AF.Exp, scale=-0.5),
                      r=("rs64", "rowmb"), w=("rs0",), tag=f"{name}_rstd{pp}")
                p.add("dve", lambda e: e.tensor_copy(rbf, mr),
                      r=("rs0",), w=("rowrb",), tag=f"{name}_rbf{pp}")
                mm(sC, onesb[0:1, :], mbf, True, True, ("rowmb", "onesb"),
                   f"{name}_bcm{pp}")
                mm(sD, onesb[0:1, :], rbf, True, True, ("rowrb", "onesb"),
                   f"{name}_bcr{pp}")
                for fc in range(FC):
                    buf = poolf[2 + fc % 2]

                    def fsub(eng, fc=fc, pp=pp, buf=buf, sC=sC):
                        return eng.tensor_tensor(
                            buf[:, :], xF[:, fc, 512 * pp:512 * (pp + 1)],
                            ps[sC][:, :], ALU.subtract)
                    p.add("dve", fsub, r=(f"xF{fc}_{pp}", f"ps{sC}"),
                          w=(f"pf{2 + fc % 2}",), tag=f"{name}_sub{pp}_{fc}")

                    def fmul(eng, fc=fc, pp=pp, buf=buf, sD=sD):
                        return eng.tensor_tensor(
                            xnsv[:, fc, 512 * pp:512 * (pp + 1)],
                            buf[:, :], ps[sD][:, :], ALU.mult)
                    p.add("dve", fmul, r=(f"pf{2 + fc % 2}", f"ps{sD}"),
                          w=(f"xn{fc}_{pp}",), tag=f"{name}_mul{pp}_{fc}")

            pl = list(pieces)
            for i in range(0, len(pl), 2):
                pair = pl[i:i + 2]
                for pp in pair:
                    stats(pp)
                for pp in pair:
                    finish(pp)

        def router(name):
            for t in range(NT):
                pp = t // 4
                for kc in range(FC):
                    mm(6, xnsv[:, kc, 128 * t:128 * (t + 1)], wrsb[:, kc, :],
                       t == 0 and kc == 0, t == NT - 1 and kc == FC - 1,
                       (f"xn{kc}_{pp}", "wrsb"),
                       f"{name}_lg{t}_{kc}", outsl=ps[6][:, 8 * t:8 * (t + 1)],
                       skipg=True)
            lgta = poolf[0][:, 0:128]
            p.add("dve", lambda e: e.tensor_tensor(
                lgta.rearrange("q (t e) -> q t e", e=E), ps[6][:, 0:128]
                .rearrange("q (t e) -> q t e", e=E),
                brbc[:, None, :].to_broadcast([128, NT, E]), ALU.add),
                r=("ps6", "brbc"), w=("pf0",), tag=f"{name}_lga")
            for t in range(NT):
                lgs = lgta[:, 8 * t:8 * (t + 1)]
                p.add("dve", lambda e, lgs=lgs: e.max(out=mxt[:, :], in_=lgs),
                      r=("pf0",), w=("mxt",), tag=f"{name}_max{t}")
                p.add("dve", lambda e: e.tensor_tensor(
                    s1c[:, 0:1], mxt[:, 0:1], mxt[:, 1:2], ALU.subtract),
                    r=("mxt",), w=("s1d",), tag=f"{name}_d{t}")
                p.add("act", lambda e: e.activation(
                    s1c[:, 1:2], s1c[:, 0:1], AF.Sigmoid),
                    r=("s1d",), w=("s1s",), tag=f"{name}_sg{t}")
                p.add("dve", lambda e: e.tensor_scalar(
                    s1c[:, 2:3], s1c[:, 1:2], 0.5, None, ALU.mult),
                    r=("s1s",), w=("s1h",), tag=f"{name}_h1{t}")
                p.add("dve", lambda e: e.tensor_scalar(
                    s1c[:, 3:4], s1c[:, 1:2], -0.5, 0.5, ALU.mult, ALU.add),
                    r=("s1s",), w=("s2h",), tag=f"{name}_h2{t}")
                p.add("dve", lambda e, lgs=lgs: e.tensor_scalar(
                    e1w[:, :], lgs, mxt[:, 0:1], s1c[:, 2:3],
                    ALU.is_equal, ALU.mult),
                    r=("pf0", "mxt", "s1h"), w=("e1w",), tag=f"{name}_e1{t}")
                p.add("dve", lambda e, lgs=lgs: e.tensor_scalar(
                    e2w[:, :], lgs, mxt[:, 1:2], s1c[:, 3:4],
                    ALU.is_equal, ALU.mult),
                    r=("pf0", "mxt", "s2h"), w=("e2w",), tag=f"{name}_e2{t}")
                p.add("dve", lambda e: e.tensor_tensor(
                    wtt[:, :], e1w[:, :], e2w[:, :], ALU.add),
                    r=("e1w", "e2w"), w=("wtt",), tag=f"{name}_wt{t}")
                bk = 2 if t % 2 == 0 else 3
                transpose(bk, wtt[:, :], eyef[:, :], ("wtt", "eyef"),
                          f"{name}_trw{t}", outsl=ps[bk][0:E, 0:128])

                def fev(eng, t=t, bk=bk):
                    return eng.tensor_copy(wbT[:, 128 * t:128 * (t + 1)],
                                           ps[bk][0:E, 0:128])
                p.add("dve", fev, r=(f"ps{bk}",), w=(f"wbT{t // 4}",),
                      tag=f"{name}_wev{t}")

        def w1_unit(e_, u):
            base = 512 * u if u < 4 else 2048 + 512 * (u - 4)
            return W1_h[e_, :, base:base + 512].rearrange(
                "(kc q) c -> q kc c", q=128)

        def w2_unit(e_, gr, gc):
            return W2_h[e_, 1024 * gr:1024 * (gr + 1),
                        512 * gc:512 * (gc + 1)].rearrange(
                "(kc q) c -> q kc c", q=128)

        def qkv_unit(u):
            return wqkv_h[:, 512 * u:512 * (u + 1)].rearrange(
                "(kc q) c -> q kc c", q=128)

        def proj_unit(u):
            return wproj_h[:, 512 * u:512 * (u + 1)].rearrange(
                "(kc q) c -> q kc c", q=128)

        def load_unit(slot, src_ap, tag):
            return p.dma("pool", slots[slot][:, :, :], src_ap,
                         w=(f"slot{slot}",), tag=tag)

        def ffn(name, slot0, fences=(), post_tb=None):
            fdep = tuple(fences)
            sched = []
            for tb in range(NTB):
                for e_ in range(E):
                    for k in range(4):
                        sched.append(("w1", e_, k, tb))
                        sched.append(("w1", e_, 4 + k, tb))
                    sched.append(("w2", e_, 0, 0, tb))
                    sched.append(("w2", e_, 1, 0, tb))
                    sched.append(("w2", e_, 0, 1, tb))
                    sched.append(("w2", e_, 1, 1, tb))
            slot_of = {}
            for idx, u in enumerate(sched):
                slot_of[u] = (idx + slot0) % 4
            ptr = [0]

            def pump(n):
                for _ in range(n):
                    if ptr[0] >= len(sched):
                        return
                    u = sched[ptr[0]]
                    sl = slot_of[u]
                    if u[0] == "w1":
                        i_ = load_unit(sl, w1_unit(u[1], u[2]),
                                       f"{name}_l1_{u[3]}_{u[1]}_{u[2]}")
                    else:
                        i_ = load_unit(sl, w2_unit(u[1], u[2], u[3]),
                                       f"{name}_l2_{u[4]}_{u[1]}_{u[2]}{u[3]}")
                    p.ops[i_]["r"] = p.ops[i_]["r"] + fdep
                    ptr[0] += 1

            pump(4)
            for tb in range(NTB):
                for e_ in range(E):
                    for j in range(NJ):
                        ua = slot_of[("w1", e_, j // 4, tb)]
                        ug = slot_of[("w1", e_, 4 + j // 4, tb)]
                        jj = j % 4
                        for p5 in range(2):
                            pp = 2 * tb + p5
                            bA = (2 * j + p5) % 2
                            bG = 2 + (2 * j + p5) % 2
                            for kc in range(FC):
                                mm(bA, slots[ua][:, kc, 128 * jj:128 * (jj + 1)],
                                   xnsv[:, kc, 512 * pp:512 * (pp + 1)],
                                   kc == 0, kc == FC - 1,
                                   (f"slot{ua}", f"xn{kc}_{pp}"),
                                   f"{name}_fa{tb}_{e_}_{j}_{p5}_{kc}")
                            for kc in range(FC):
                                mm(bG, slots[ug][:, kc, 128 * jj:128 * (jj + 1)],
                                   xnsv[:, kc, 512 * pp:512 * (pp + 1)],
                                   kc == 0, kc == FC - 1,
                                   (f"slot{ug}", f"xn{kc}_{pp}"),
                                   f"{name}_fg{tb}_{e_}_{j}_{p5}_{kc}")
                            tA = poolb[p5]
                            tG = poolb[2 + p5]

                            def fsi(eng, bA=bA, e_=e_, j=j, tA=tA):
                                return eng.activation(
                                    tA[:, :], ps[bA][:, :], AF.Silu,
                                    bias=b1T[:, e_, j:j + 1])
                            p.add("act", fsi, r=(f"ps{bA}", f"b1T{j}"),
                                  w=(f"pb{p5}",),
                                  tag=f"{name}_si{tb}_{e_}_{j}_{p5}")

                            def fg(eng, bG=bG, e_=e_, j=j, tG=tG):
                                return eng.tensor_scalar(
                                    tG[:, :], ps[bG][:, :],
                                    b1T[:, e_, 16 + j:17 + j], None, ALU.add)
                            p.add("dve", fg, r=(f"ps{bG}", f"b1T{16 + j}"),
                                  w=(f"pb{2 + p5}",),
                                  tag=f"{name}_g{tb}_{e_}_{j}_{p5}")

                            def fac(eng, j=j, p5=p5, tA=tA, tG=tG):
                                return eng.tensor_tensor(
                                    actb[:, j, 512 * p5:512 * (p5 + 1)],
                                    tA[:, :], tG[:, :], ALU.mult)
                            p.add("pool", fac,
                                  r=(f"pb{p5}", f"pb{2 + p5}") + fdep,
                                  w=(f"act{j}_{p5}",),
                                  tag=f"{name}_ac{tb}_{e_}_{j}_{p5}")
                        if j % 4 == 3:
                            pump(2)
                    for p5 in range(2):
                        pp = 2 * tb + p5
                        wb = poolf[2 * (e_ % 2) + p5]
                        bW = 6 if p5 == 0 else 7
                        mm(bW, esel[:, 128 * e_:128 * (e_ + 1)],
                           wbT[:, 512 * pp:512 * (pp + 1)], True, True,
                           ("esel", f"wbT{pp}"), f"{name}_wbc{tb}_{e_}_{p5}")

                        def fwb(eng, wb=wb, bW=bW):
                            return eng.copy(wb[:, :], ps[bW][:, :])
                        p.add("act", fwb, r=(f"ps{bW}",),
                              w=(f"pf{2 * (e_ % 2) + p5}",),
                              tag=f"{name}_wbe{tb}_{e_}_{p5}")
                    for m_ in range(FC):
                        for p5 in range(2):
                            pp = 2 * tb + p5
                            bO = 4 + (2 * m_ + p5) % 2
                            for j in range(NJ):
                                us = slot_of[("w2", e_, j // 8, m_ // 4, tb)]
                                mm(bO, slots[us][:, j % 8,
                                                 128 * (m_ % 4):128 * (m_ % 4 + 1)],
                                   actb[:, j, 512 * p5:512 * (p5 + 1)],
                                   j == 0, j == NJ - 1,
                                   (f"slot{us}", f"act{j}_{p5}"),
                                   f"{name}_f2{tb}_{e_}_{m_}_{p5}_{j}")
                            tmp = poolf[4 + (2 * m_ + p5) % 2]
                            wbk = f"pf{2 * (e_ % 2) + p5}"

                            def fom(eng, bO=bO, tmp=tmp, e_=e_, p5=p5):
                                return eng.tensor_tensor(
                                    tmp[:, :], ps[bO][:, :],
                                    poolf[2 * (e_ % 2) + p5][:, :], ALU.mult)
                            p.add("dve", fom, r=(f"ps{bO}", wbk),
                                  w=(f"pf{4 + (2 * m_ + p5) % 2}",),
                                  tag=f"{name}_om{tb}_{e_}_{m_}_{p5}")

                            def fax(eng, m_=m_, pp=pp, tmp=tmp):
                                return eng.tensor_tensor(
                                    xF[:, m_, 512 * pp:512 * (pp + 1)],
                                    xF[:, m_, 512 * pp:512 * (pp + 1)],
                                    tmp[:, :], ALU.add)
                            p.add("pool", fax,
                                  r=(f"pf{4 + (2 * m_ + p5) % 2}",
                                     f"xF{m_}_{pp}") + fdep,
                                  w=(f"xF{m_}_{pp}",),
                                  tag=f"{name}_ax{tb}_{e_}_{m_}_{p5}")
                        if m_ in (3, 7):
                            pump(2)
                for p5 in range(2):
                    pp = 2 * tb + p5
                    for m_ in range(FC):
                        bB = 6 if m_ % 2 == 0 else 7
                        mm(bB, b2sb[:, 128 * m_:128 * (m_ + 1)],
                           wbT[:, 512 * pp:512 * (pp + 1)], True, True,
                           ("b2sb", f"wbT{pp}"), f"{name}_b2{pp}_{m_}")

                        def fb2(eng, m_=m_, pp=pp, bB=bB):
                            return eng.tensor_tensor(
                                xF[:, m_, 512 * pp:512 * (pp + 1)],
                                xF[:, m_, 512 * pp:512 * (pp + 1)],
                                ps[bB][:, :], ALU.add)
                        p.add("dve", fb2, r=(f"ps{bB}", f"xF{m_}_{pp}"),
                              w=(f"xF{m_}_{pp}",), tag=f"{name}_b2a{pp}_{m_}")
                if post_tb is not None:
                    post_tb(tb)

        def lm_piece(pp):
            for fc in range(FC):
                def fr(eng, fc=fc, pp=pp):
                    return eng.reduce_sum(
                        xlm[:, fc, 4 * pp:4 * (pp + 1)],
                        xnsv[:, fc, 512 * pp:512 * (pp + 1)].rearrange(
                            "q (l s) -> q l s", s=128),
                        axis=AX.X)
                p.add("dve", fr, r=(f"xn{fc}_{pp}",), w=(f"xlm{fc}",),
                      tag=f"lm_r{fc}_{pp}")

        def pre_attn():
            for fc in range(FC):
                bk = 6 if fc % 2 == 0 else 7
                transpose(bk, xlm[:, fc, :], eyef[:, :], (f"xlm{fc}", "eyef"),
                          f"lm_t{fc}", outsl=ps[bk][0:LML, 0:128])

                def fe(eng, fc=fc, bk=bk):
                    return eng.activation(xlmT[:, 128 * fc:128 * (fc + 1)],
                                          ps[bk][0:LML, 0:128], AF.Copy,
                                          scale=1.0 / 128)
                p.add("act", fe, r=(f"ps{bk}",), w=("xlmT",), tag=f"lm_e{fc}")
            p.dma("pool", cc1_in[:, :], xlmT[:, :], r=("xlmT",), w=("cc1_in",),
                  tag="cc1_put")

            def fcc1(eng):
                return eng.collective_compute(
                    "AllGather", ALU.bypass,
                    replica_groups=[[0, 1, 2, 3], [4, 5, 6, 7]],
                    ins=[cc1_in.ap().opt()], outs=[cc1_out.ap().opt()])
            p.add("pool", fcc1, r=("cc1_in",), w=("cc1_out",), kind="cc",
                  tag="cc1")
            load_unit(0, qkv_unit(2), "ld_k0")
            load_unit(1, qkv_unit(3), "ld_k1")
            load_unit(2, qkv_unit(0), "ld_q0")
            load_unit(3, qkv_unit(1), "ld_q1")

        def attention():
            def qT_block(hc, extra=()):
                us = 2 + hc // 4
                for pp in range(NP):
                    bk = 0 if pp % 2 == 0 else 1
                    for kc in range(FC):
                        mm(bk, slots[us][:, kc, 128 * (hc % 4):128 * (hc % 4 + 1)],
                           xnsv[:, kc, 512 * pp:512 * (pp + 1)],
                           kc == 0, kc == FC - 1,
                           (f"slot{us}", f"xn{kc}_{pp}"), f"qT_m{hc}_{pp}")

                    def fe(eng, hc=hc, pp=pp, bk=bk):
                        return eng.activation(
                            E1[:, hc, 512 * pp:512 * (pp + 1)],
                            ps[bk][:, :], AF.Copy, scale=0.125)
                    p.add("act", fe, r=(f"ps{bk}", "m1fence") + extra,
                          w=(f"qE{hc}_{pp}",), tag=f"qT_e{hc}_{pp}")

            def e1_block(hc):
                for pp in range(NP):
                    bk = 2 if pp % 2 == 0 else 3
                    for par in range(2):
                        mm(bk, klmT[64 * par:64 * (par + 1), hc, :],
                           E1[64 * par:64 * (par + 1), hc,
                              512 * pp:512 * (pp + 1)],
                           True, True, (f"klmT{hc}", f"qE{hc}_{pp}"),
                           f"e1_m{2 * hc + par}_{pp}",
                           outsl=ps[bk][64 * par:64 * (par + 1), :],
                           tile=(64 * par, 64 * par))

                    def fx(eng, hc=hc, pp=pp, bk=bk):
                        return eng.activation(
                            E1[:, hc, 512 * pp:512 * (pp + 1)],
                            ps[bk][:, :], AF.Exp)
                    p.add("act", fx, r=(f"ps{bk}",),
                          w=(f"E1_{2 * hc}_{pp}", f"E1_{2 * hc + 1}_{pp}",
                             f"qE{hc}_{pp}"),
                          tag=f"e1_x{hc}_{pp}")

            for hc in range(7):
                qT_block(hc)
            p.dma("pool", LMf[:, :], cc1_out[:, :],
                  r=("cc1_out", "m1fence"), w=("LMf",), tag="cc1_get")
            for fc in range(FC):
                bk = 6 if fc % 2 == 0 else 7
                transpose(bk, LMf[:, 128 * fc:128 * (fc + 1)],
                          eyef[0:64, 0:64], ("LMf", "eyef"), f"lmf_t{fc}",
                          outsl=ps[bk][:, 0:64])

                def fe(eng, fc=fc, bk=bk):
                    return eng.tensor_copy(xlmF[:, fc, :], ps[bk][:, 0:64])
                p.add("dve", fe, r=(f"ps{bk}", "LMf"), w=("xlmF",),
                      tag=f"lmf_e{fc}")
            for (dst, dk, bs, scale, tg) in ((qlmT, "qlmT", 2, 0.125, "qlm"),
                                             (klmT, "klmT", 0, 1.0, "klm")):
                for hc in range(HC):
                    us = bs + hc // 4
                    bk = 6 if hc % 2 == 0 else 7
                    for kc in range(FC):
                        mm(bk, slots[us][:, kc, 128 * (hc % 4):128 * (hc % 4 + 1)],
                           xlmF[:, kc, :], kc == 0, kc == FC - 1,
                           (f"slot{us}", "xlmF"), f"{tg}_m{hc}_{kc}",
                           outsl=ps[bk][:, 0:LMG])

                    def fe(eng, dst=dst, hc=hc, bk=bk, scale=scale):
                        return eng.activation(dst[:, hc, :], ps[bk][:, 0:LMG],
                                              AF.Copy, scale=scale)
                    p.add("act", fe, r=(f"ps{bk}",), w=(dk + str(hc),),
                          tag=f"{tg}_e{hc}")

            TP0, TP1 = (0, 0), (64, 64)

            def pmm(bank, lhsT_t, rhs_t, rk, tag, n=64):
                mm(bank, lhsT_t[0:64, :], rhs_t[0:64, 0:n], True, True, rk,
                   tag + "_a", outsl=ps[bank][0:64, 0:n], tile=TP0)
                return mm(bank, lhsT_t[64:128, :], rhs_t[64:128, 0:n], True,
                          True, rk, tag + "_b",
                          outsl=ps[bank][64:128, 0:n], tile=TP1)

            def inv_pair(hc):
                sl = hc % 2
                invs = invs2[sl]
                pa, pb = (4, 5) if sl == 0 else (5, 4)
                K2 = invs[:, 0, :]
                K2T = invs[:, 1, :]
                V = invs[:, 2, :]
                VT = invs[:, 3, :]
                Wm = invs[:, 4, :]
                WT = invs[:, 5, :]
                Pm = invs[:, 6, :]
                E2 = invs[:, 7, :]
                r0 = invs[:, 6, 0:1]
                r1 = invs[:, 6, 1:2]
                r2 = invs[:, 6, 2:3]
                r3 = invs[:, 6, 3:4]
                pmm(pa, qlmT[:, hc, :], klmT[:, hc, :],
                    (f"qlmT{hc}", f"klmT{hc}"), f"iv_s2_{hc}")
                p.add("act", lambda e, E2=E2, pa=pa: e.activation(
                    E2, ps[pa][:, 0:64], AF.Exp),
                    r=(f"ps{pa}",), w=(f"ivE2{sl}",), tag=f"iv_e2_{hc}")
                p.add("dve", lambda e, E2=E2, r0=r0: e.reduce_sum(
                    r0, E2, axis=AX.X),
                    r=(f"ivE2{sl}",), w=(f"ivr0{sl}",), tag=f"iv_rs_{hc}")
                p.add("dve", lambda e, r0=r0, r1=r1: e.reciprocal(r1, r0),
                      r=(f"ivr0{sl}",), w=(f"ivr1{sl}",), tag=f"iv_rr_{hc}")
                p.add("dve", lambda e, K2=K2, E2=E2, r1=r1: e.tensor_scalar(
                    K2, E2, r1, None, ALU.mult),
                    r=(f"ivE2{sl}", f"ivr1{sl}"), w=(f"ivK2{sl}",), tag=f"iv_k2_{hc}")
                pmm(pb, K2, eyeS[:, :], (f"ivK2{sl}", "eyeS"), f"iv_k2t_{hc}")
                p.add("dve", lambda e, K2T=K2T, pb=pb: e.tensor_copy(
                    K2T, ps[pb][:, 0:64]),
                    r=(f"ps{pb}",), w=(f"ivK2T{sl}",), tag=f"iv_k2te_{hc}")
                pmm(pa, onesf[:, :], K2, (f"ivK2{sl}", "onesf"), f"iv_cs_{hc}")
                p.add("dve", lambda e, r2=r2, pa=pa: e.reduce_max(
                    r2, ps[pa][:, 0:64], axis=AX.X),
                    r=(f"ps{pa}",), w=(f"ivr2{sl}",), tag=f"iv_dm_{hc}")
                p.add("dve", lambda e, r2=r2, r3=r3: e.reciprocal(r3, r2),
                      r=(f"ivr2{sl}",), w=(f"ivr3{sl}",), tag=f"iv_dr_{hc}")
                p.add("dve", lambda e, V=V, K2T=K2T, r3=r3: e.tensor_scalar(
                    V, K2T, r3, None, ALU.mult),
                    r=(f"ivK2T{sl}", f"ivr3{sl}"), w=(f"ivV{sl}",), tag=f"iv_v0_{hc}")
                for it in range(6):
                    pmm(pb, V, eyeS[:, :], (f"ivV{sl}", "eyeS"), f"iv_vt_{hc}_{it}")
                    p.add("dve", lambda e, VT=VT, pb=pb: e.tensor_copy(
                        VT, ps[pb][:, 0:64]),
                        r=(f"ps{pb}",), w=(f"ivVT{sl}",), tag=f"iv_vte_{hc}_{it}")
                    pmm(pa, K2T, V, (f"ivK2T{sl}", f"ivV{sl}"), f"iv_w_{hc}_{it}")
                    p.add("dve", lambda e, Wm=Wm, pa=pa: e.tensor_copy(
                        Wm, ps[pa][:, 0:64]),
                        r=(f"ps{pa}",), w=(f"ivW{sl}",), tag=f"iv_we_{hc}_{it}")
                    pmm(pb, Wm, eyeS[:, :], (f"ivW{sl}", "eyeS"), f"iv_wt_{hc}_{it}")
                    p.add("dve", lambda e, WT=WT, pb=pb: e.tensor_copy(
                        WT, ps[pb][:, 0:64]),
                        r=(f"ps{pb}",), w=(f"ivWT{sl}",), tag=f"iv_wte_{hc}_{it}")
                    p.add("dve", lambda e, Pm=Pm, Wm=Wm: e.scalar_tensor_tensor(
                        Pm, eyeS[:, :], 7.0, Wm, ALU.mult, ALU.subtract),
                        r=(f"ivW{sl}", "eyeS", f"ivr0{sl}", f"ivr1{sl}", f"ivr2{sl}", f"ivr3{sl}"),
                        w=(f"ivP{sl}",), tag=f"iv_p1_{hc}_{it}")
                    pmm(pa, WT, Pm, (f"ivWT{sl}", f"ivP{sl}"), f"iv_p2m_{hc}_{it}")
                    p.add("dve", lambda e, Pm=Pm, pa=pa: e.scalar_tensor_tensor(
                        Pm, eyeS[:, :], 15.0, ps[pa][:, 0:64], ALU.mult,
                        ALU.subtract),
                        r=(f"ps{pa}", "eyeS", f"ivP{sl}"), w=(f"ivP{sl}",),
                        tag=f"iv_p2_{hc}_{it}")
                    pmm(pa, WT, Pm, (f"ivWT{sl}", f"ivP{sl}"), f"iv_p3m_{hc}_{it}")
                    p.add("dve", lambda e, Pm=Pm, pa=pa: e.scalar_tensor_tensor(
                        Pm, eyeS[:, :], 13.0, ps[pa][:, 0:64], ALU.mult,
                        ALU.subtract),
                        r=(f"ps{pa}", "eyeS", f"ivP{sl}"), w=(f"ivP{sl}",),
                        tag=f"iv_p3_{hc}_{it}")
                    pmm(pa, VT, Pm, (f"ivVT{sl}", f"ivP{sl}"), f"iv_vm_{hc}_{it}")
                    p.add("act", lambda e, V=V, pa=pa: e.activation(
                        V, ps[pa][:, 0:64], AF.Copy, scale=0.25),
                        r=(f"ps{pa}",), w=(f"ivV{sl}",), tag=f"iv_vn_{hc}_{it}")
                pmm(pb, V, eyeS[:, :], (f"ivV{sl}", "eyeS"), f"iv_ft_{hc}")

                def ff(eng, hc=hc, pb=pb):
                    return eng.tensor_copy(invT[:, hc, :], ps[pb][:, 0:64])
                p.add("dve", ff, r=(f"ps{pb}",), w=(f"invT{hc}",),
                      tag=f"iv_out_{hc}")

            for hc in range(HC):
                if hc == 7:
                    qT_block(7, extra=("xlmF",))
                e1_block(hc)
                inv_pair(hc)

            p.fence("F_attns", extra_w=("xlmT",))
            load_unit(2, qkv_unit(4), "ld_v0")
            load_unit(3, qkv_unit(5), "ld_v1")
            kT2 = attn_s[:, 0:1024].rearrange("q (hh t) -> q hh t", hh=4)
            vt = [attn_s[:, 1024 + 512 * i:1024 + 512 * (i + 1)]
                  for i in range(2)]

            def b1_half(half):
                ku = half
                vu = 2 + half
                for tq in range(8):
                    for hq in range(4):
                        bk = 2 if hq % 2 == 0 else 3
                        for kc in range(FC):
                            mm(bk, slots[ku][:, kc, 128 * hq:128 * (hq + 1)],
                               xnsv[:, kc, 256 * tq:256 * (tq + 1)],
                               kc == 0, kc == FC - 1,
                               (f"slot{ku}", f"xn{kc}_{tq // 2}"),
                               f"b1_km{half}_{tq}_{hq}_{kc}",
                               outsl=ps[bk][:, 0:256])

                        def fk(eng, hq=hq, bk=bk):
                            return eng.activation(kT2[:, hq, :],
                                                  ps[bk][:, 0:256], AF.Copy)
                        p.add("act", fk, r=(f"ps{bk}", "F_attns"),
                              w=(f"kT{hq}",), tag=f"b1_ke{half}_{tq}_{hq}")
                    for tt in range(2):
                        t = 2 * tq + tt
                        tk = tt
                        bkv = 0 if t % 2 == 0 else 1
                        for kc in range(FC):
                            mm(bkv, xnsv[:, kc, 128 * t:128 * (t + 1)],
                               slots[vu][:, kc, :], kc == 0, kc == FC - 1,
                               (f"slot{vu}", f"xn{kc}_{t // 4}"),
                               f"b1_vm{half}_{t}_{kc}")

                        def fv(eng, tk=tk, bkv=bkv):
                            return eng.activation(vt[tk][:, :], ps[bkv][:, :],
                                                  AF.Copy)
                        p.add("act", fv, r=(f"ps{bkv}", "F_attns"),
                              w=(f"vt{tk}",), tag=f"b1_ve{half}_{t}")
                        for hh in range(8):
                            h = 8 * half + hh
                            hc, par = h // 2, h % 2
                            hq = hc - 4 * half
                            bk = 4 if hh % 2 == 0 else 5
                            mm(bk, kT2[64 * par:64 * (par + 1), hq,
                                       128 * tt:128 * (tt + 1)],
                               qlmT[64 * par:64 * (par + 1), hc, :],
                               True, True, (f"kT{hq}", f"qlmT{hc}"),
                               f"b1_s3{half}_{t}_{hh}", outsl=ps[bk][:, 0:HD])

                            def f3(eng, hh=hh, bk=bk):
                                return eng.activation(e3t[:, hh, :],
                                                      ps[bk][:, 0:HD], AF.Exp)
                            p.add("act", f3, r=(f"ps{bk}",), w=(f"e3{hh}",),
                                  tag=f"b1_x3{half}_{t}_{hh}")
                            hpl = h // 2 - 4 * half
                            bsv = 6 + half
                            first = (t == 0 and hh == 0)
                            last = (t == NT - 1 and hh == 7)
                            mm(bsv, e3t[:, hh, :],
                               vt[tk][:, 64 * (2 * hq + par):
                                      64 * (2 * hq + par + 1)],
                               first, last, (f"e3g{g4}" if False else f"e3{hh}", f"vt{tk}"),
                               f"b1_sv{half}_{t}_{hh}",
                               outsl=ps[bsv][64 * par:64 * (par + 1),
                                             64 * hpl:64 * (hpl + 1)],
                               skipg=True)
                            mm(bsv, e3t[:, hh, :], onesb[:, 0:1],
                               False, last, (f"e3{hh}", "onesb"),
                               f"b1_z{half}_{t}_{hh}",
                               outsl=ps[bsv][64 * par:64 * (par + 1),
                                             256 + hpl:256 + hpl + 1],
                               skipg=True)
            pay_t = [cpay[:, 0:520], svz[:, 0:520]]
            sz_t = [cpay[:, 520:1040], svz[:, 520:1040]]
            pay_k = ["cpay0", "pay1"]
            sz_k = ["svz0", "svz1"]
            fence_k = ["F_sA", "F_sB"]

            def ship_half(half):
                bsv = 6 + half
                pay = pay_t[half]
                for hh in range(8):
                    h = 8 * half + hh
                    par, hpl = h % 2, h // 2 - 4 * half

                    def fs(eng, pay=pay, hh=hh, par=par, hpl=hpl, bsv=bsv):
                        return eng.tensor_copy(
                            pay[:, 64 * hh:64 * (hh + 1)],
                            ps[bsv][64 * par:64 * (par + 1),
                                    64 * hpl:64 * (hpl + 1)])
                    p.add("dve", fs, r=(f"ps{bsv}", fence_k[half]),
                          w=(pay_k[half],), tag=f"sv_e{half}_{hh}")

                    def fz(eng, pay=pay, hh=hh, par=par, hpl=hpl, bsv=bsv):
                        return eng.tensor_copy(
                            pay[:, 512 + hh:513 + hh],
                            ps[bsv][64 * par:64 * (par + 1),
                                    256 + hpl:256 + hpl + 1])
                    p.add("dve", fz, r=(f"ps{bsv}", fence_k[half]),
                          w=(pay_k[half],), tag=f"svz_e{half}_{hh}")
                p.dma("pool", cc2_in[half][:, :], pay, r=(pay_k[half],),
                      w=(f"cc2i{half}",), tag=f"cc2_put{half}")

                def fcc2(eng, half=half):
                    return eng.collective_compute(
                        "AllGather", ALU.bypass,
                        replica_groups=[[0, 1, 2, 3], [4, 5, 6, 7]],
                        ins=[cc2_in[half].ap().opt()],
                        outs=[cc2_out[half].ap().opt()])
                p.add("pool", fcc2, r=(f"cc2i{half}",), w=(f"cc2o{half}",),
                      kind="cc", tag=f"cc2_{half}")

            def combine_half(half):
                sz = sz_t[half]
                pay = pay_t[half]
                p.dma("pool", sz, cc2_out[half][0:64, :],
                      r=(f"cc2o{half}", fence_k[half]), w=(sz_k[half],),
                      tag=f"cc2_g0_{half}")
                for r_ in range(1, 4):
                    p.dma("pool", pay, cc2_out[half][64 * r_:64 * (r_ + 1), :],
                          r=(f"cc2o{half}", pay_k[half]), w=(pay_k[half],),
                          tag=f"cc2_g{r_}_{half}")

                    def fa(eng, sz=sz, pay=pay):
                        return eng.tensor_tensor(sz, sz, pay, ALU.add)
                    p.add("dve", fa, r=(sz_k[half], pay_k[half]),
                          w=(sz_k[half],), tag=f"cc2_a{r_}_{half}")

            def gsv_half(half):
                sz = sz_t[half]

                def frz(eng, half=half, sz=sz):
                    return eng.activation(rzh[:, 8 * half:8 * (half + 1)],
                                          sz[:, 512:520], AF.Ln)
                p.add("act", frz, r=(sz_k[half],), w=(f"rzh{half}",),
                      tag=f"rz_ln{half}")

                def frx(eng, half=half):
                    return eng.activation(rzh[:, 8 * half:8 * (half + 1)],
                                          rzh[:, 8 * half:8 * (half + 1)],
                                          AF.Exp, scale=-1.0)
                p.add("act", frx, r=(f"rzh{half}",), w=(f"rzh{half}",),
                      tag=f"rz_ex{half}")
                for hc in range(4 * half, 4 * half + 4):
                    for par in range(2):
                        h = 2 * hc + par
                        hh = h - 8 * half

                        def fn_(eng, h=h, sz=sz, hh=hh, par=par):
                            return eng.tensor_scalar(
                                sv3n[64 * par:64 * (par + 1), :],
                                sz[:, 64 * hh:64 * (hh + 1)],
                                rzh[:, h:h + 1], None, ALU.mult)
                        p.add("dve", fn_, r=(sz_k[half], f"rzh{half}"),
                              w=("sv3n",), tag=f"sv3n{h}")
                    bk = 4 if hc % 2 == 0 else 5
                    for par in range(2):
                        mm(bk, invT[64 * par:64 * (par + 1), hc, :],
                           sv3n[64 * par:64 * (par + 1), :], True, True,
                           (f"invT{hc}", "sv3n"), f"g_m{hc}_{par}",
                           outsl=ps[bk][64 * par:64 * (par + 1), 0:64],
                           tile=(64 * par, 64 * par))

                    def fg(eng, hc=hc, bk=bk):
                        return eng.tensor_copy(Gsb[:, hc, :],
                                               ps[bk][:, 0:64])
                    p.add("dve", fg, r=(f"ps{bk}",), w=(f"G{hc}",),
                          tag=f"g_e{hc}")
                for hc in range(4 * half, 4 * half + 4):
                    for pp in range(NP):
                        b1_ = 0 if pp % 2 == 0 else 1
                        b2_ = 2 if pp % 2 == 0 else 3
                        for par in range(2):
                            h = 2 * hc + par
                            mm(b1_, Gsb[64 * par:64 * (par + 1), hc, :],
                               E1[64 * par:64 * (par + 1), hc,
                                  512 * pp:512 * (pp + 1)],
                               True, True, (f"G{hc}", f"E1_{h}_{pp}"),
                               f"sv_m{h}_{pp}",
                               outsl=ps[b1_][64 * par:64 * (par + 1), :],
                               tile=(64 * par, 64 * par))
                            mm(b2_, onesb[64 * par:64 * (par + 1), 0:64],
                               E1[64 * par:64 * (par + 1), hc,
                                  512 * pp:512 * (pp + 1)],
                               True, True, (f"E1_{h}_{pp}", "onesb"),
                               f"svn_m{h}_{pp}",
                               outsl=ps[b2_][64 * par:64 * (par + 1), :],
                               tile=(64 * par, 64 * par))
                        pr = (4, 5) if (4 * hc + pp) % 2 == 0 else (2, 3)
                        lnb = poolf[pr[0]]
                        rzb = poolf[pr[1]]

                        def fl(eng, b2_=b2_, lnb=lnb):
                            return eng.activation(lnb[:, :], ps[b2_][:, :],
                                                  AF.Ln)
                        p.add("act", fl, r=(f"ps{b2_}",), w=(f"pf{pr[0]}",),
                              tag=f"sv_l{hc}_{pp}")

                        def fx(eng, lnb=lnb, rzb=rzb):
                            return eng.activation(rzb[:, :], lnb[:, :],
                                                  AF.Exp, scale=-1.0)
                        p.add("act", fx, r=(f"pf{pr[0]}",), w=(f"pf{pr[1]}",),
                              tag=f"sv_x{hc}_{pp}")

                        def fm(eng, hc=hc, pp=pp, b1_=b1_, rzb=rzb):
                            return eng.tensor_tensor(
                                xnsv[:, hc, 512 * pp:512 * (pp + 1)],
                                ps[b1_][:, :], rzb[:, :], ALU.mult)
                        p.add("dve", fm, r=(f"ps{b1_}", f"pf{pr[1]}"),
                              w=(f"xn{hc}_{pp}",), tag=f"sv_w{hc}_{pp}")

            b1_half(0)
            p.fence("F_sA", extra_w=("slot2",))
            ship_half(0)
            combine_half(0)
            b1_half(1)
            p.fence("F_sB", extra_w=("slot3",))
            ship_half(1)
            load_unit(0, proj_unit(0), "ld_pj0")
            load_unit(1, proj_unit(1), "ld_pj1")
            gsv_half(0)
            combine_half(1)
            gsv_half(1)
            for pp in range(NP):
                for m_ in range(FC):
                    bk = 4 + (pp * FC + m_) % 3
                    us = m_ // 4
                    for kc in range(FC):
                        mm(bk, slots[us][:, kc, 128 * (m_ % 4):128 * (m_ % 4 + 1)],
                           xnsv[:, kc, 512 * pp:512 * (pp + 1)],
                           kc == 0, kc == FC - 1,
                           (f"slot{us}", f"xn{kc}_{pp}"), f"pj_m{pp}_{m_}")

                    def fr_(eng, m_=m_, pp=pp, bk=bk):
                        return eng.tensor_tensor(
                            xF[:, m_, 512 * pp:512 * (pp + 1)],
                            xF[:, m_, 512 * pp:512 * (pp + 1)],
                            ps[bk][:, :], ALU.add)
                    p.add("dve", fr_, r=(f"ps{bk}", f"xF{m_}_{pp}"),
                          w=(f"xF{m_}_{pp}",), tag=f"pj_r{pp}_{m_}")

        def output():
            for hp_ in range(8):
                hsel = hp_ % 2
                pp = hp_ // 2
                for tt in range(2):
                    t = 2 * hp_ + tt
                    for fc in range(FC):
                        bk = (tt * FC + fc) % 4
                        mm(bk, xF[:, fc, 128 * t:128 * (t + 1)],
                           eyef[:, :], True, True, (f"xF{fc}_{pp}", "eyef"),
                           f"tro{t}_{fc}", outsl=ps[bk][:, 0:128])

                        def fe(eng, hsel=hsel, tt=tt, fc=fc, bk=bk):
                            return eng.copy(
                                stg[:, 2 * hsel + tt, 128 * fc:128 * (fc + 1)],
                                ps[bk][:, 0:128])
                        p.add("act", fe, r=(f"ps{bk}",),
                              w=(f"ostg{hp_}", f"stgh{hsel}",
                                 f"slot{2 + hsel}", "cpay0", "pay1",
                                 "svz0", "svz1"),
                              tag=f"evo{t}_{fc}")
                p.dma("sp", y_h[256 * hp_:256 * (hp_ + 1), :].rearrange(
                    "(tp q) f -> q tp f", q=128),
                    stg[:, 2 * hsel:2 * hsel + 2, :],
                    r=(f"ostg{hp_}", f"stgh{hsel}", f"slot{2 + hsel}",
                       "cpay0", "pay1", "svz0", "svz1"),
                    w=(f"ost{hp_}",), tag=f"st_y{hp_}")

        has_attn = "attn" in phases

        def f1_post(tb):
            layer_norm("ln2", pieces=(2 * tb, 2 * tb + 1))
            lm_piece(2 * tb)
            lm_piece(2 * tb + 1)
            if tb == NTB - 1:
                pre_attn()
        if "ffn1" in phases:
            layer_norm("ln1")
            router("r1")
            ffn("f1", slot0=2, post_tb=f1_post if has_attn else None)
        if has_attn:
            if "ffn1" not in phases:
                layer_norm("ln2")
                for pp in range(NP):
                    lm_piece(pp)
                pre_attn()
            p.fence("m1fence",
                    extra_w=tuple(f"act{j}_{p5}" for j in range(NJ)
                                  for p5 in range(2)))
            attention()
        if "ffn2" in phases:
            layer_norm("ln3")
            router("r2")
            p.fence("m2fence",
                    extra_w=tuple(f"E1_{h}_{pp}" for h in range(H)
                                  for pp in range(NP)))
            p.fence("F2_slots", extra_w=("cpay0", "pay1", "svz0", "svz1"))
            ffn("f2", slot0=2, fences=("m2fence", "F2_slots"))
        output()

        p.finalize_and_emit(block, sems)
    return nc


_cache = {}


def _get_program(phases):
    key = tuple(phases)
    if key not in _cache:
        _cache[key] = build_program(phases)
    return _cache[key]


def _consts():
    eye = np.eye(128, dtype=np.float32)
    eyes = np.tile(np.eye(64, dtype=np.float32), (2, 1))
    esel = np.zeros((E, 1024), dtype=ml_dtypes.bfloat16)
    for e_ in range(E):
        esel[e_, 128 * e_:128 * (e_ + 1)] = 1.0
    return eye, eyes, esel


def kernel(L0_attn_x, L0_attn_w_qkv, L0_attn_w_proj, L0_ffn_W1, L0_ffn_b1,
           L0_ffn_W2, L0_ffn_b2, L0_ffn_Wr, L0_ffn_br):
    x = np.ascontiguousarray(np.asarray(L0_attn_x, dtype=np.float32))
    xf = x.reshape(2 * 8192, C)
    eye, eyes, esel_np = _consts()
    common = {
        "w_qkv": np.ascontiguousarray(np.asarray(L0_attn_w_qkv, np.float32)),
        "w_proj": np.ascontiguousarray(np.asarray(L0_attn_w_proj, np.float32)),
        "W1": np.ascontiguousarray(np.asarray(L0_ffn_W1, np.float32)),
        "b1": np.ascontiguousarray(np.asarray(L0_ffn_b1, np.float32)),
        "W2": np.ascontiguousarray(np.asarray(L0_ffn_W2, np.float32)),
        "b2": np.ascontiguousarray(np.asarray(L0_ffn_b2, np.float32)),
        "Wr": np.ascontiguousarray(np.asarray(L0_ffn_Wr, np.float32)),
        "br": np.ascontiguousarray(
            np.asarray(L0_ffn_br, np.float32)).reshape(1, E),
        "eye_f": eye,
        "eye64s": eyes,
        "esel": esel_np,
    }
    in_maps = []
    for c in range(NCORE):
        m = dict(common)
        m["x"] = np.ascontiguousarray(xf[T * c:T * (c + 1)])
        in_maps.append(m)
    nc = _get_program(PHASES)
    res = run_bass_kernel_spmd(nc, in_maps, core_ids=list(range(NCORE)))
    out = np.concatenate([np.asarray(r["y"], dtype=np.float32)
                          for r in res.results], axis=0)
    return out.reshape(2, 8192, C)


# version 24
# speedup vs baseline: 1.0065x; 1.0065x over previous
import numpy as np
import ml_dtypes
import concourse.bass as bass
import concourse.mybir as mybir
from concourse.bass_utils import run_bass_kernel_spmd

import concourse.bass_utils as _bu

if not getattr(_bu, "_ldwopt_patched", False):
    _orig_run_command = _bu.run_command

    def _patched_run_command(argv, **kwargs):
        argv = [a.replace("--enable-ldw-opt=false", "--enable-ldw-opt=true")
                if isinstance(a, str) else a for a in argv]
        return _orig_run_command(argv, **kwargs)

    _bu.run_command = _patched_run_command
    _bu._ldwopt_patched = True

F32 = mybir.dt.float32
BF16 = mybir.dt.bfloat16
AF = mybir.ActivationFunctionType
ALU = mybir.AluOpType
AX = mybir.AxisListType

NCORE = 8
T = 2048
C = 1024
FC = 8
NP = 4
NT = 16
H = 16
HD = 64
HC = 8
LMG = 64
LML = 16
E = 8
NJ = 16
NTB = 2

PHASES = ("ffn1", "attn", "ffn2")


class Prog:
    ENGINES = ("pe", "act", "dve", "pool", "sp")

    def __init__(self):
        self.ops = []

    def add(self, engine, fn, r=(), w=(), kind="c", tag=""):
        self.ops.append(dict(e=engine, fn=fn, r=tuple(r), w=tuple(w),
                             kind=kind, tag=tag, marked=False, token=None))
        return len(self.ops) - 1

    def dma(self, queue, out, in_, r=(), w=(), tag=""):
        def fn(eng, out=out, in_=in_):
            return eng.dma_start(out=out, in_=in_)
        return self.add(queue, fn, r=r, w=w, kind="d", tag=tag)

    def fence(self, key, extra_w=(), tag="fence"):
        return self.add("sp", lambda eng: eng.nop(), r=(),
                        w=(key,) + tuple(extra_w), kind="c", tag=tag)

    def finalize_and_emit(self, block, sems):
        ops = self.ops
        last_writer, readers = {}, {}
        deps_all = []
        for i, op in enumerate(ops):
            deps = set()
            for k in op["r"]:
                if k in last_writer:
                    deps.add(last_writer[k])
            for k in op["w"]:
                if k in last_writer:
                    deps.add(last_writer[k])
                for j in readers.get(k, ()):
                    deps.add(j)
            deps.discard(i)
            deps_all.append(deps)
            for k in op["r"]:
                readers.setdefault(k, []).append(i)
            for k in op["w"]:
                last_writer[k] = i
                readers[k] = []
        for i, op in enumerate(ops):
            for d in deps_all[i]:
                if ops[d]["e"] == "pe" and op["e"] == "pe" \
                        and ops[d]["kind"] == "c":
                    continue
                ops[d]["marked"] = True
        ccount = {e: 0 for e in self.ENGINES}
        dcount = {"sp": 0, "pool": 0}
        nccc = 0
        for op in ops:
            if op["kind"] == "d":
                k = dcount[op["e"]]
                dcount[op["e"]] += 1
                op["token"] = (f"{op['e']}_d{k % 4}", 16 * (k // 4 + 1))
                op["dpred"] = (f"{op['e']}_d{k % 4}", 16 * (k // 4)) \
                    if k >= 4 else None
            elif op["kind"] == "cc":
                op["token"] = (f"cc{nccc}", 1)
                nccc += 1
            elif op["marked"]:
                ccount[op["e"]] += 1
                op["token"] = (op["e"], ccount[op["e"]])
        self.final_counts = {e: ccount[e] for e in self.ENGINES}
        for q in ("sp", "pool"):
            for i4 in range(4):
                n = dcount[q] // 4 + (1 if dcount[q] % 4 > i4 else 0)
                self.final_counts[f"{q}_d{i4}"] = 16 * n
        for i in range(nccc):
            self.final_counts[f"cc{i}"] = 1
        per_engine = {e: [] for e in self.ENGINES}
        for i, op in enumerate(ops):
            per_engine[op["e"]].append(i)

        def emit_for(engine, eng):
            last_waited = {}
            for i in per_engine[engine]:
                op = ops[i]
                waits = {}
                for d in deps_all[i]:
                    dop = ops[d]
                    if dop["kind"] == "c" and dop["e"] == "pe" \
                            and engine == "pe":
                        continue
                    tok = dop["token"]
                    assert tok is not None, (ops[d]["tag"], op["tag"])
                    s, v = tok
                    waits[s] = max(waits.get(s, 0), v)
                for s, v in sorted(waits.items()):
                    if last_waited.get(s, 0) < v:
                        eng.wait_ge(sems[s], v)
                        last_waited[s] = v
                if op["kind"] == "d" and op.get("dpred"):
                    ss, vv = op["dpred"]
                    if last_waited.get(ss, 0) < vv:
                        eng.wait_ge(sems[ss], vv)
                        last_waited[ss] = vv
                inst = op["fn"](eng)
                if op["kind"] == "d":
                    inst.then_inc(sems[op["token"][0]], 16)
                elif op["kind"] == "cc":
                    inst.then_inc(sems[op["token"][0]], 1)
                elif op["marked"]:
                    inst.then_inc(sems[op["e"]], 1)

        @block.tensor
        def _(eng):
            emit_for("pe", eng)

        @block.scalar
        def _(eng):
            emit_for("act", eng)

        @block.vector
        def _(eng):
            emit_for("dve", eng)

        @block.gpsimd
        def _(eng):
            emit_for("pool", eng)

        fc_ = self.final_counts

        @block.sync
        def _(eng):
            emit_for("sp", eng)
            for s, v in sorted(fc_.items()):
                if v and s in sems:
                    eng.wait_ge(sems[s], v)


def build_program(phases=PHASES):
    nc = bass.Bass("TRN2", target_bir_lowering=False, debug=False)
    p = Prog()

    x_h = nc.dram_tensor("x", [T, C], F32, kind="ExternalInput")
    wqkv_h = nc.dram_tensor("w_qkv", [C, 3 * C], F32, kind="ExternalInput")
    wproj_h = nc.dram_tensor("w_proj", [C, C], F32, kind="ExternalInput")
    W1_h = nc.dram_tensor("W1", [E, C, 4096], F32, kind="ExternalInput")
    b1_h = nc.dram_tensor("b1", [E, 4096], F32, kind="ExternalInput")
    W2_h = nc.dram_tensor("W2", [E, 2048, C], F32, kind="ExternalInput")
    b2_h = nc.dram_tensor("b2", [E, C], F32, kind="ExternalInput")
    Wr_h = nc.dram_tensor("Wr", [C, E], F32, kind="ExternalInput")
    br_h = nc.dram_tensor("br", [1, E], F32, kind="ExternalInput")
    eyef_h = nc.dram_tensor("eye_f", [128, 128], F32, kind="ExternalInput")
    eyes_h = nc.dram_tensor("eye64s", [128, 64], F32, kind="ExternalInput")
    esel_h = nc.dram_tensor("esel", [E, 1024], BF16, kind="ExternalInput")
    y_h = nc.dram_tensor("y", [T, C], F32, kind="ExternalOutput")

    cc1_in = nc.dram_tensor("cc1_in", [LML, C], BF16, kind="Internal")
    cc1_out = nc.dram_tensor("cc1_out", [LMG, C], BF16, kind="Internal")
    cc2_in = [nc.dram_tensor(f"cc2_in{i}", [64, 520], F32, kind="Internal")
              for i in range(2)]
    cc2_out = [nc.dram_tensor(f"cc2_out{i}", [256, 520], F32, kind="Internal")
               for i in range(2)]

    from contextlib import ExitStack
    with ExitStack() as ctx:
        def sb(name, shape, dtype):
            return ctx.enter_context(nc.sbuf_tensor(name, shape, dtype))

        slot_arena = nc.alloc_sbuf_tensor("slot_arena", [128, 16384], BF16)
        sa_addr = nc.lookup_mloc(slot_arena).addr
        slots = [nc.alloc_sbuf_tensor_at(f"slot{i}", [128, 8, 512], BF16,
                                         offset=sa_addr + 8192 * i)
                 for i in range(4)]
        stg = nc.alloc_sbuf_tensor_at("stg", [128, 4, 1024], F32,
                                      offset=sa_addr + 16384)
        cpay = nc.alloc_sbuf_tensor_at("cpay", [64, 1040], F32,
                                       offset=sa_addr + 16384)
        svz = nc.alloc_sbuf_tensor_at("svz", [64, 1040], F32,
                                      offset=sa_addr + 24576)
        m1_arena = nc.alloc_sbuf_tensor("m1_arena", [128, 16384], BF16)
        m1_addr = nc.lookup_mloc(m1_arena).addr
        E1 = nc.alloc_sbuf_tensor_at("E1", [128, HC, T], BF16, offset=m1_addr)
        actb = nc.alloc_sbuf_tensor_at("actb", [128, NJ, 1024], BF16,
                                       offset=m1_addr)
        LMf = nc.alloc_sbuf_tensor_at("LMf", [64, 1024], F32,
                                      offset=m1_addr + 7 * 4096)
        attn_s = sb("attn_s", [128, 2048], BF16)

        xF = sb("xF", [128, FC, T], F32)
        xnsv = sb("xnsv", [128, FC, T], BF16)
        poolf = [sb(f"pf{i}", [128, 512], F32) for i in range(6)]
        poolb = [sb(f"pb{i}", [128, 512], BF16) for i in range(4)]
        e3t = sb("e3t", [128, 8, HD], BF16)
        b1raw = sb("b1raw", [E, 256], F32)
        rowp = sb("rowp", [128, 512], F32)
        rowb = sb("rowb", [1, 1024], BF16)
        invs2 = [sb(f"invs{i}", [128, 8, 64], F32) for i in range(2)]
        invT = sb("invT", [128, HC, 64], BF16)
        Gsb = sb("Gsb", [128, HC, 64], BF16)
        qlmT = sb("qlmT", [128, HC, LMG], BF16)
        klmT = sb("klmT", [128, HC, LMG], BF16)
        xlmF = sb("xlmF", [128, FC, LMG], BF16)
        xlm = sb("xlm", [128, FC, LML], F32)
        b2sb = sb("b2sb", [E, C], BF16)
        wrsb = sb("wrsb", [128, FC, E], BF16)
        brbc = sb("brbc", [128, E], F32)
        brraw = sb("brraw", [1, E], F32)
        brbf = sb("brbf", [1, E], BF16)
        wbT = sb("wbT", [E, T], BF16)
        eyef = sb("eyef", [128, 128], F32)
        eyeS = sb("eyeS", [128, 64], F32)
        onesf = sb("onesf", [128, 64], F32)
        onesb = sb("onesb", [128, 128], BF16)
        esel = sb("esel_sb", [E, 1024], BF16)
        lgt = sb("lgt", [128, E], F32)
        mxt = sb("mxt", [128, E], F32)
        e1w = sb("e1w", [128, E], F32)
        e2w = sb("e2w", [128, E], F32)
        s1c = sb("s1c", [128, 4], F32)
        wtt = sb("wtt", [128, E], F32)
        rzh = sb("rzh", [64, H], F32)
        epsb = sb("epsb", [128, 1], F32)
        sv3n = sb("sv3n", [128, 64], BF16)
        xlmT = attn_s[0:LML, 0:1024]

        ps = {i: ctx.enter_context(nc.psum_tensor(f"pbank{i}", [128, 512], F32))
              for i in range(8)}

        sems = {}
        for name in ("pe", "act", "dve", "pool", "sp",
                     "sp_d0", "sp_d1", "sp_d2", "sp_d3",
                     "pool_d0", "pool_d1", "pool_d2", "pool_d3",
                     "cc0", "cc1", "cc2", "cc3"):
            sems[name] = ctx.enter_context(nc.semaphore(name + "_s"))
        block = ctx.enter_context(nc.Block())

        def mm(bank, lhsT, rhs, start, stop, rkeys, tag, outsl=None,
               extra_w=(), tile=None, skipg=False):
            out = ps[bank][:, :] if outsl is None else outsl

            def fn(eng, out=out, lhsT=lhsT, rhs=rhs, start=start, stop=stop,
                   tile=tile, skipg=skipg):
                return eng.matmul(out, lhsT, rhs, start=start, stop=stop,
                                  tile_position=tile, skip_group_check=skipg)
            r = tuple(rkeys) + (() if start else (f"ps{bank}",))
            return p.add("pe", fn, r=r, w=(f"ps{bank}",) + tuple(extra_w),
                         tag=tag)

        def transpose(bank, in_, ident, rkeys, tag, outsl):
            def fn(eng, outsl=outsl, in_=in_, ident=ident):
                return eng.transpose(outsl, in_, ident)
            return p.add("pe", fn, r=tuple(rkeys), w=(f"ps{bank}",), tag=tag)

        p.dma("sp", eyef[:, :], eyef_h[:, :], w=("eyef",), tag="ld_eye")
        p.dma("sp", eyeS[:, :], eyes_h[:, :], w=("eyeS",), tag="ld_eyes")
        for hp_ in range(8):
            hsel = hp_ % 2
            hkey = f"stgh{hsel}"
            p.dma("sp", stg[:, 2 * hsel:2 * hsel + 2, :],
                  x_h[256 * hp_:256 * (hp_ + 1), :].rearrange(
                      "(tp q) f -> q tp f", q=128),
                  r=(), w=(hkey, f"slot{2 + hsel}"), tag=f"ld_x{hp_}")
            pp = hp_ // 2
            for tt in range(2):
                t = 2 * hp_ + tt
                for fc in range(FC):
                    bk = 6 if (tt * FC + fc) % 2 == 0 else 7
                    transpose(bk, stg[:, 2 * hsel + tt, 128 * fc:128 * (fc + 1)],
                              eyef[:, :], (hkey, f"slot{2 + hsel}", "eyef"),
                              f"tr_x{t}_{fc}", outsl=ps[bk][:, 0:128])

                    def fev(eng, fc=fc, t=t, bk=bk):
                        return eng.copy(xF[:, fc, 128 * t:128 * (t + 1)],
                                        ps[bk][:, 0:128])
                    p.add("act", fev, r=(f"ps{bk}", hkey),
                          w=(f"xF{fc}_{pp}",), tag=f"ev_x{t}_{fc}")

        p.dma("sp", esel[:, :], esel_h[:, :], w=("esel",), tag="ld_esel")
        p.add("pool", lambda e: e.memset(onesf[:, :], 1.0), w=("onesf",),
              tag="ones_f")
        p.add("pool", lambda e: e.memset(onesb[:, :], 1.0), w=("onesb",),
              tag="ones_b")
        p.add("pool", lambda e: e.memset(epsb[:, :], 1e-5), w=("epsb",),
              tag="eps_b")
        p.dma("pool", b2sb[:, :], b2_h[:, :], w=("b2sb",), tag="ld_b2")
        p.dma("pool", wrsb[:, :, :],
              Wr_h.ap().rearrange("(kc q) e -> q kc e", q=128),
              w=("wrsb",), tag="ld_wr")
        p.dma("sp", brraw[:, :], br_h[:, :], w=("brraw",), tag="ld_br")
        b1T = sb("b1T", [128, E, 32], F32)
        for piece in range(16):
            p.dma("sp", b1raw[:, :], b1_h[:, 256 * piece:256 * (piece + 1)],
                  r=("b1raw",), w=("b1raw",), tag=f"ld_b1_{piece}")
            for jj in range(2):
                j = 2 * piece + jj
                bk = 6 if j % 2 == 0 else 7
                transpose(bk, b1raw[0:E, 128 * jj:128 * (jj + 1)],
                          eyef[0:E, 0:E], ("b1raw", "eyef"), f"tr_b1_{j}",
                          outsl=ps[bk][:, 0:E])

                def fev(eng, j=j, bk=bk):
                    return eng.tensor_copy(b1T[:, :, j], ps[bk][:, 0:E])
                p.add("dve", fev, r=(f"ps{bk}", "b1raw"), w=(f"b1T{j}",),
                      tag=f"ev_b1_{j}")
        p.add("dve", lambda e: e.tensor_copy(brbf[:, :], brraw[:, :]),
              r=("brraw",), w=("brbf",), tag="br_bf")
        mm(6, onesb[0:1, :], brbf[:, :], True, True, ("onesb", "brbf"),
           "br_bc_mm", outsl=ps[6][:, 0:E])
        p.add("act", lambda e: e.copy(brbc[:, :], ps[6][:, 0:E]),
              r=("ps6",), w=("brbc",), tag="br_bc_ev")

        def layer_norm(name, pieces=tuple(range(NP))):
            mr = rowp[0:1, :]
            vv = rowp[32:33, :]
            lv = rowp[64:65, :]
            mbf = rowb[:, 0:512]
            rbf = rowb[:, 512:1024]

            def stats(pp):
                sA, sB = (0, 1) if pp % 2 == 0 else (4, 5)
                for fc in range(FC):
                    xq = poolb[2 + fc % 2]
                    mm(sA, onesf[:, 0:1], xF[:, fc, 512 * pp:512 * (pp + 1)],
                       fc == 0, fc == FC - 1, (f"xF{fc}_{pp}", "onesf"),
                       f"{name}_sum{pp}_{fc}", outsl=ps[sA][0:1, :])

                    def fsq(eng, fc=fc, pp=pp, xq=xq):
                        return eng.activation(
                            xq[:, :], xF[:, fc, 512 * pp:512 * (pp + 1)],
                            AF.Square)
                    p.add("act", fsq, r=(f"xF{fc}_{pp}",),
                          w=(f"pb{2 + fc % 2}",), tag=f"{name}_sq{pp}_{fc}")
                    mm(sB, onesb[:, 0:1], xq[:, :], fc == 0, fc == FC - 1,
                       (f"pb{2 + fc % 2}", "onesb"), f"{name}_ssq{pp}_{fc}",
                       outsl=ps[sB][0:1, :])

            def finish(pp):
                sA, sB = (0, 1) if pp % 2 == 0 else (4, 5)
                sC, sD = (2, 3) if pp % 2 == 0 else (6, 7)
                p.add("dve", lambda e, sA=sA: e.tensor_scalar(
                    mr, ps[sA][0:1, :], 1.0 / C, None, ALU.mult),
                    r=(f"ps{sA}",), w=("rs0",), tag=f"{name}_mean{pp}")
                p.add("dve", lambda e: e.tensor_copy(mbf, mr),
                      r=("rs0",), w=("rowmb",), tag=f"{name}_mbf{pp}")
                p.add("dve", lambda e: e.tensor_tensor(vv, mr, mr, ALU.mult),
                      r=("rs0",), w=("rs32",), tag=f"{name}_msq{pp}")
                p.add("dve", lambda e, sB=sB: e.scalar_tensor_tensor(
                    vv, ps[sB][0:1, :], 1.0 / C, vv, ALU.mult, ALU.subtract),
                    r=(f"ps{sB}", "rs32"), w=("rs32",), tag=f"{name}_var{pp}")
                p.add("act", lambda e: e.activation(lv, vv, AF.Ln,
                                                    bias=epsb[0:1, 0:1]),
                      r=("rs32", "epsb"), w=("rs64",), tag=f"{name}_lnv{pp}")
                p.add("act", lambda e: e.activation(mr, lv, AF.Exp, scale=-0.5),
                      r=("rs64", "rowmb"), w=("rs0",), tag=f"{name}_rstd{pp}")
                p.add("dve", lambda e: e.tensor_copy(rbf, mr),
                      r=("rs0",), w=("rowrb",), tag=f"{name}_rbf{pp}")
                mm(sC, onesb[0:1, :], mbf, True, True, ("rowmb", "onesb"),
                   f"{name}_bcm{pp}")
                mm(sD, onesb[0:1, :], rbf, True, True, ("rowrb", "onesb"),
                   f"{name}_bcr{pp}")
                for fc in range(FC):
                    buf = poolf[2 + fc % 2]

                    def fsub(eng, fc=fc, pp=pp, buf=buf, sC=sC):
                        return eng.tensor_tensor(
                            buf[:, :], xF[:, fc, 512 * pp:512 * (pp + 1)],
                            ps[sC][:, :], ALU.subtract)
                    p.add("dve", fsub, r=(f"xF{fc}_{pp}", f"ps{sC}"),
                          w=(f"pf{2 + fc % 2}",), tag=f"{name}_sub{pp}_{fc}")

                    def fmul(eng, fc=fc, pp=pp, buf=buf, sD=sD):
                        return eng.tensor_tensor(
                            xnsv[:, fc, 512 * pp:512 * (pp + 1)],
                            buf[:, :], ps[sD][:, :], ALU.mult)
                    p.add("dve", fmul, r=(f"pf{2 + fc % 2}", f"ps{sD}"),
                          w=(f"xn{fc}_{pp}",), tag=f"{name}_mul{pp}_{fc}")

            pl = list(pieces)
            for i in range(0, len(pl), 2):
                pair = pl[i:i + 2]
                for pp in pair:
                    stats(pp)
                for pp in pair:
                    finish(pp)

        def router(name):
            for t in range(NT):
                pp = t // 4
                for kc in range(FC):
                    mm(6, xnsv[:, kc, 128 * t:128 * (t + 1)], wrsb[:, kc, :],
                       t == 0 and kc == 0, t == NT - 1 and kc == FC - 1,
                       (f"xn{kc}_{pp}", "wrsb"),
                       f"{name}_lg{t}_{kc}", outsl=ps[6][:, 8 * t:8 * (t + 1)],
                       skipg=True)
            lgta = poolf[0][:, 0:128]
            p.add("dve", lambda e: e.tensor_tensor(
                lgta.rearrange("q (t e) -> q t e", e=E), ps[6][:, 0:128]
                .rearrange("q (t e) -> q t e", e=E),
                brbc[:, None, :].to_broadcast([128, NT, E]), ALU.add),
                r=("ps6", "brbc"), w=("pf0",), tag=f"{name}_lga")
            for t in range(NT):
                lgs = lgta[:, 8 * t:8 * (t + 1)]
                p.add("dve", lambda e, lgs=lgs: e.max(out=mxt[:, :], in_=lgs),
                      r=("pf0",), w=("mxt",), tag=f"{name}_max{t}")
                p.add("dve", lambda e: e.tensor_tensor(
                    s1c[:, 0:1], mxt[:, 0:1], mxt[:, 1:2], ALU.subtract),
                    r=("mxt",), w=("s1d",), tag=f"{name}_d{t}")
                p.add("act", lambda e: e.activation(
                    s1c[:, 1:2], s1c[:, 0:1], AF.Sigmoid),
                    r=("s1d",), w=("s1s",), tag=f"{name}_sg{t}")
                p.add("dve", lambda e: e.tensor_scalar(
                    s1c[:, 2:3], s1c[:, 1:2], 0.5, None, ALU.mult),
                    r=("s1s",), w=("s1h",), tag=f"{name}_h1{t}")
                p.add("dve", lambda e: e.tensor_scalar(
                    s1c[:, 3:4], s1c[:, 1:2], -0.5, 0.5, ALU.mult, ALU.add),
                    r=("s1s",), w=("s2h",), tag=f"{name}_h2{t}")
                p.add("dve", lambda e, lgs=lgs: e.tensor_scalar(
                    e1w[:, :], lgs, mxt[:, 0:1], s1c[:, 2:3],
                    ALU.is_equal, ALU.mult),
                    r=("pf0", "mxt", "s1h"), w=("e1w",), tag=f"{name}_e1{t}")
                p.add("dve", lambda e, lgs=lgs: e.tensor_scalar(
                    e2w[:, :], lgs, mxt[:, 1:2], s1c[:, 3:4],
                    ALU.is_equal, ALU.mult),
                    r=("pf0", "mxt", "s2h"), w=("e2w",), tag=f"{name}_e2{t}")
                p.add("dve", lambda e: e.tensor_tensor(
                    wtt[:, :], e1w[:, :], e2w[:, :], ALU.add),
                    r=("e1w", "e2w"), w=("wtt",), tag=f"{name}_wt{t}")
                bk = 2 if t % 2 == 0 else 3
                transpose(bk, wtt[:, :], eyef[:, :], ("wtt", "eyef"),
                          f"{name}_trw{t}", outsl=ps[bk][0:E, 0:128])

                def fev(eng, t=t, bk=bk):
                    return eng.tensor_copy(wbT[:, 128 * t:128 * (t + 1)],
                                           ps[bk][0:E, 0:128])
                p.add("dve", fev, r=(f"ps{bk}",), w=(f"wbT{t // 4}",),
                      tag=f"{name}_wev{t}")

        def w1_unit(e_, u):
            base = 512 * u if u < 4 else 2048 + 512 * (u - 4)
            return W1_h[e_, :, base:base + 512].rearrange(
                "(kc q) c -> q kc c", q=128)

        def w2_unit(e_, gr, gc):
            return W2_h[e_, 1024 * gr:1024 * (gr + 1),
                        512 * gc:512 * (gc + 1)].rearrange(
                "(kc q) c -> q kc c", q=128)

        def qkv_unit(u):
            return wqkv_h[:, 512 * u:512 * (u + 1)].rearrange(
                "(kc q) c -> q kc c", q=128)

        def proj_unit(u):
            return wproj_h[:, 512 * u:512 * (u + 1)].rearrange(
                "(kc q) c -> q kc c", q=128)

        def load_unit(slot, src_ap, tag):
            return p.dma("pool", slots[slot][:, :, :], src_ap,
                         w=(f"slot{slot}",), tag=tag)

        def ffn(name, slot0, fences=(), post_tb=None, tail_units=()):
            fdep = tuple(fences)
            sched = []
            for tb in range(NTB):
                for e_ in range(E):
                    for k in range(4):
                        sched.append(("w1", e_, k, tb))
                        sched.append(("w1", e_, 4 + k, tb))
                    sched.append(("w2", e_, 0, 0, tb))
                    sched.append(("w2", e_, 1, 0, tb))
                    sched.append(("w2", e_, 0, 1, tb))
                    sched.append(("w2", e_, 1, 1, tb))
            for ti in range(len(tail_units)):
                sched.append(("tail", ti))
            slot_of = {}
            for idx, u in enumerate(sched):
                slot_of[u] = (idx + slot0) % 4
            ptr = [0]

            def pump(n):
                for _ in range(n):
                    if ptr[0] >= len(sched):
                        return
                    u = sched[ptr[0]]
                    sl = slot_of[u]
                    if u[0] == "w1":
                        i_ = load_unit(sl, w1_unit(u[1], u[2]),
                                       f"{name}_l1_{u[3]}_{u[1]}_{u[2]}")
                    elif u[0] == "w2":
                        i_ = load_unit(sl, w2_unit(u[1], u[2], u[3]),
                                       f"{name}_l2_{u[4]}_{u[1]}_{u[2]}{u[3]}")
                    else:
                        src, tag = tail_units[u[1]]
                        i_ = load_unit(sl, src, tag)
                    p.ops[i_]["r"] = p.ops[i_]["r"] + fdep
                    ptr[0] += 1

            pump(4)
            for tb in range(NTB):
                for e_ in range(E):
                    for j in range(NJ):
                        ua = slot_of[("w1", e_, j // 4, tb)]
                        ug = slot_of[("w1", e_, 4 + j // 4, tb)]
                        jj = j % 4
                        for p5 in range(2):
                            pp = 2 * tb + p5
                            bA = (2 * j + p5) % 2
                            bG = 2 + (2 * j + p5) % 2
                            for kc in range(FC):
                                mm(bA, slots[ua][:, kc, 128 * jj:128 * (jj + 1)],
                                   xnsv[:, kc, 512 * pp:512 * (pp + 1)],
                                   kc == 0, kc == FC - 1,
                                   (f"slot{ua}", f"xn{kc}_{pp}"),
                                   f"{name}_fa{tb}_{e_}_{j}_{p5}_{kc}")
                            for kc in range(FC):
                                mm(bG, slots[ug][:, kc, 128 * jj:128 * (jj + 1)],
                                   xnsv[:, kc, 512 * pp:512 * (pp + 1)],
                                   kc == 0, kc == FC - 1,
                                   (f"slot{ug}", f"xn{kc}_{pp}"),
                                   f"{name}_fg{tb}_{e_}_{j}_{p5}_{kc}")
                            tA = poolb[p5]
                            tG = poolb[2 + p5]

                            def fsi(eng, bA=bA, e_=e_, j=j, tA=tA):
                                return eng.activation(
                                    tA[:, :], ps[bA][:, :], AF.Silu,
                                    bias=b1T[:, e_, j:j + 1])
                            p.add("act", fsi, r=(f"ps{bA}", f"b1T{j}"),
                                  w=(f"pb{p5}",),
                                  tag=f"{name}_si{tb}_{e_}_{j}_{p5}")

                            def fg(eng, bG=bG, e_=e_, j=j, tG=tG):
                                return eng.tensor_scalar(
                                    tG[:, :], ps[bG][:, :],
                                    b1T[:, e_, 16 + j:17 + j], None, ALU.add)
                            p.add("dve", fg, r=(f"ps{bG}", f"b1T{16 + j}"),
                                  w=(f"pb{2 + p5}",),
                                  tag=f"{name}_g{tb}_{e_}_{j}_{p5}")

                            def fac(eng, j=j, p5=p5, tA=tA, tG=tG):
                                return eng.tensor_tensor(
                                    actb[:, j, 512 * p5:512 * (p5 + 1)],
                                    tA[:, :], tG[:, :], ALU.mult)
                            p.add("pool", fac,
                                  r=(f"pb{p5}", f"pb{2 + p5}") + fdep,
                                  w=(f"act{j}_{p5}",),
                                  tag=f"{name}_ac{tb}_{e_}_{j}_{p5}")
                        if j % 4 == 3:
                            pump(2)
                    for p5 in range(2):
                        pp = 2 * tb + p5
                        wb = poolf[2 * (e_ % 2) + p5]
                        bW = 6 if p5 == 0 else 7
                        mm(bW, esel[:, 128 * e_:128 * (e_ + 1)],
                           wbT[:, 512 * pp:512 * (pp + 1)], True, True,
                           ("esel", f"wbT{pp}"), f"{name}_wbc{tb}_{e_}_{p5}")

                        def fwb(eng, wb=wb, bW=bW):
                            return eng.copy(wb[:, :], ps[bW][:, :])
                        p.add("act", fwb, r=(f"ps{bW}",),
                              w=(f"pf{2 * (e_ % 2) + p5}",),
                              tag=f"{name}_wbe{tb}_{e_}_{p5}")
                    for m_ in range(FC):
                        for p5 in range(2):
                            pp = 2 * tb + p5
                            bO = 4 + (2 * m_ + p5) % 2
                            for j in range(NJ):
                                us = slot_of[("w2", e_, j // 8, m_ // 4, tb)]
                                mm(bO, slots[us][:, j % 8,
                                                 128 * (m_ % 4):128 * (m_ % 4 + 1)],
                                   actb[:, j, 512 * p5:512 * (p5 + 1)],
                                   j == 0, j == NJ - 1,
                                   (f"slot{us}", f"act{j}_{p5}"),
                                   f"{name}_f2{tb}_{e_}_{m_}_{p5}_{j}")
                            tmp = poolf[4 + (2 * m_ + p5) % 2]
                            wbk = f"pf{2 * (e_ % 2) + p5}"

                            def fom(eng, bO=bO, tmp=tmp, e_=e_, p5=p5):
                                return eng.tensor_tensor(
                                    tmp[:, :], ps[bO][:, :],
                                    poolf[2 * (e_ % 2) + p5][:, :], ALU.mult)
                            p.add("dve", fom, r=(f"ps{bO}", wbk),
                                  w=(f"pf{4 + (2 * m_ + p5) % 2}",),
                                  tag=f"{name}_om{tb}_{e_}_{m_}_{p5}")

                            def fax(eng, m_=m_, pp=pp, tmp=tmp):
                                return eng.tensor_tensor(
                                    xF[:, m_, 512 * pp:512 * (pp + 1)],
                                    xF[:, m_, 512 * pp:512 * (pp + 1)],
                                    tmp[:, :], ALU.add)
                            p.add("pool", fax,
                                  r=(f"pf{4 + (2 * m_ + p5) % 2}",
                                     f"xF{m_}_{pp}") + fdep,
                                  w=(f"xF{m_}_{pp}",),
                                  tag=f"{name}_ax{tb}_{e_}_{m_}_{p5}")
                        if m_ in (3, 7):
                            pump(2)
                for p5 in range(2):
                    pp = 2 * tb + p5
                    for m_ in range(FC):
                        bB = 6 if m_ % 2 == 0 else 7
                        mm(bB, b2sb[:, 128 * m_:128 * (m_ + 1)],
                           wbT[:, 512 * pp:512 * (pp + 1)], True, True,
                           ("b2sb", f"wbT{pp}"), f"{name}_b2{pp}_{m_}")

                        def fb2(eng, m_=m_, pp=pp, bB=bB):
                            return eng.tensor_tensor(
                                xF[:, m_, 512 * pp:512 * (pp + 1)],
                                xF[:, m_, 512 * pp:512 * (pp + 1)],
                                ps[bB][:, :], ALU.add)
                        p.add("dve", fb2, r=(f"ps{bB}", f"xF{m_}_{pp}"),
                              w=(f"xF{m_}_{pp}",), tag=f"{name}_b2a{pp}_{m_}")
                if post_tb is not None:
                    post_tb(tb)
            pump(len(tail_units))

        def lm_piece(pp):
            for fc in range(FC):
                def fr(eng, fc=fc, pp=pp):
                    return eng.reduce_sum(
                        xlm[:, fc, 4 * pp:4 * (pp + 1)],
                        xnsv[:, fc, 512 * pp:512 * (pp + 1)].rearrange(
                            "q (l s) -> q l s", s=128),
                        axis=AX.X)
                p.add("dve", fr, r=(f"xn{fc}_{pp}",), w=(f"xlm{fc}",),
                      tag=f"lm_r{fc}_{pp}")

        def pre_attn():
            for fc in range(FC):
                bk = 6 if fc % 2 == 0 else 7
                transpose(bk, xlm[:, fc, :], eyef[:, :], (f"xlm{fc}", "eyef"),
                          f"lm_t{fc}", outsl=ps[bk][0:LML, 0:128])

                def fe(eng, fc=fc, bk=bk):
                    return eng.activation(xlmT[:, 128 * fc:128 * (fc + 1)],
                                          ps[bk][0:LML, 0:128], AF.Copy,
                                          scale=1.0 / 128)
                p.add("act", fe, r=(f"ps{bk}",), w=("xlmT",), tag=f"lm_e{fc}")
            p.dma("pool", cc1_in[:, :], xlmT[:, :], r=("xlmT",), w=("cc1_in",),
                  tag="cc1_put")

            def fcc1(eng):
                return eng.collective_compute(
                    "AllGather", ALU.bypass,
                    replica_groups=[[0, 1, 2, 3], [4, 5, 6, 7]],
                    ins=[cc1_in.ap().opt()], outs=[cc1_out.ap().opt()])
            p.add("pool", fcc1, r=("cc1_in",), w=("cc1_out",), kind="cc",
                  tag="cc1")

        def attention():
            def qT_block(hc, extra=()):
                us = 2 + hc // 4
                for pp in range(NP):
                    bk = 0 if pp % 2 == 0 else 1
                    for kc in range(FC):
                        mm(bk, slots[us][:, kc, 128 * (hc % 4):128 * (hc % 4 + 1)],
                           xnsv[:, kc, 512 * pp:512 * (pp + 1)],
                           kc == 0, kc == FC - 1,
                           (f"slot{us}", f"xn{kc}_{pp}"), f"qT_m{hc}_{pp}")

                    def fe(eng, hc=hc, pp=pp, bk=bk):
                        return eng.activation(
                            E1[:, hc, 512 * pp:512 * (pp + 1)],
                            ps[bk][:, :], AF.Copy, scale=0.125)
                    p.add("act", fe, r=(f"ps{bk}", "m1fence") + extra,
                          w=(f"qE{hc}_{pp}",), tag=f"qT_e{hc}_{pp}")

            def e1_block(hc):
                for pp in range(NP):
                    bk = 2 if pp % 2 == 0 else 3
                    for par in range(2):
                        mm(bk, klmT[64 * par:64 * (par + 1), hc, :],
                           E1[64 * par:64 * (par + 1), hc,
                              512 * pp:512 * (pp + 1)],
                           True, True, (f"klmT{hc}", f"qE{hc}_{pp}"),
                           f"e1_m{2 * hc + par}_{pp}",
                           outsl=ps[bk][64 * par:64 * (par + 1), :],
                           tile=(64 * par, 64 * par))

                    def fx(eng, hc=hc, pp=pp, bk=bk):
                        return eng.activation(
                            E1[:, hc, 512 * pp:512 * (pp + 1)],
                            ps[bk][:, :], AF.Exp)
                    p.add("act", fx, r=(f"ps{bk}",),
                          w=(f"E1_{2 * hc}_{pp}", f"E1_{2 * hc + 1}_{pp}",
                             f"qE{hc}_{pp}"),
                          tag=f"e1_x{hc}_{pp}")

            for hc in range(7):
                qT_block(hc)
            p.dma("pool", LMf[:, :], cc1_out[:, :],
                  r=("cc1_out", "m1fence"), w=("LMf",), tag="cc1_get")
            for fc in range(FC):
                bk = 6 if fc % 2 == 0 else 7
                transpose(bk, LMf[:, 128 * fc:128 * (fc + 1)],
                          eyef[0:64, 0:64], ("LMf", "eyef"), f"lmf_t{fc}",
                          outsl=ps[bk][:, 0:64])

                def fe(eng, fc=fc, bk=bk):
                    return eng.tensor_copy(xlmF[:, fc, :], ps[bk][:, 0:64])
                p.add("dve", fe, r=(f"ps{bk}", "LMf"), w=("xlmF",),
                      tag=f"lmf_e{fc}")
            for (dst, dk, bs, scale, tg) in ((qlmT, "qlmT", 2, 0.125, "qlm"),
                                             (klmT, "klmT", 0, 1.0, "klm")):
                for hc in range(HC):
                    us = bs + hc // 4
                    bk = 6 if hc % 2 == 0 else 7
                    for kc in range(FC):
                        mm(bk, slots[us][:, kc, 128 * (hc % 4):128 * (hc % 4 + 1)],
                           xlmF[:, kc, :], kc == 0, kc == FC - 1,
                           (f"slot{us}", "xlmF"), f"{tg}_m{hc}_{kc}",
                           outsl=ps[bk][:, 0:LMG])

                    def fe(eng, dst=dst, hc=hc, bk=bk, scale=scale):
                        return eng.activation(dst[:, hc, :], ps[bk][:, 0:LMG],
                                              AF.Copy, scale=scale)
                    p.add("act", fe, r=(f"ps{bk}",), w=(dk + str(hc),),
                          tag=f"{tg}_e{hc}")

            TP0, TP1 = (0, 0), (64, 64)

            def pmm(bank, lhsT_t, rhs_t, rk, tag, n=64):
                mm(bank, lhsT_t[0:64, :], rhs_t[0:64, 0:n], True, True, rk,
                   tag + "_a", outsl=ps[bank][0:64, 0:n], tile=TP0)
                return mm(bank, lhsT_t[64:128, :], rhs_t[64:128, 0:n], True,
                          True, rk, tag + "_b",
                          outsl=ps[bank][64:128, 0:n], tile=TP1)

            def inv_pair(hc):
                sl = hc % 2
                invs = invs2[sl]
                pa, pb = (4, 5) if sl == 0 else (5, 4)
                K2 = invs[:, 0, :]
                K2T = invs[:, 1, :]
                V = invs[:, 2, :]
                VT = invs[:, 3, :]
                Wm = invs[:, 4, :]
                WT = invs[:, 5, :]
                Pm = invs[:, 6, :]
                E2 = invs[:, 7, :]
                r0 = invs[:, 6, 0:1]
                r1 = invs[:, 6, 1:2]
                r2 = invs[:, 6, 2:3]
                r3 = invs[:, 6, 3:4]
                pmm(pa, qlmT[:, hc, :], klmT[:, hc, :],
                    (f"qlmT{hc}", f"klmT{hc}"), f"iv_s2_{hc}")
                p.add("act", lambda e, E2=E2, pa=pa: e.activation(
                    E2, ps[pa][:, 0:64], AF.Exp),
                    r=(f"ps{pa}",), w=(f"ivE2{sl}",), tag=f"iv_e2_{hc}")
                p.add("dve", lambda e, E2=E2, r0=r0: e.reduce_sum(
                    r0, E2, axis=AX.X),
                    r=(f"ivE2{sl}",), w=(f"ivr0{sl}",), tag=f"iv_rs_{hc}")
                p.add("dve", lambda e, r0=r0, r1=r1: e.reciprocal(r1, r0),
                      r=(f"ivr0{sl}",), w=(f"ivr1{sl}",), tag=f"iv_rr_{hc}")
                p.add("dve", lambda e, K2=K2, E2=E2, r1=r1: e.tensor_scalar(
                    K2, E2, r1, None, ALU.mult),
                    r=(f"ivE2{sl}", f"ivr1{sl}"), w=(f"ivK2{sl}",), tag=f"iv_k2_{hc}")
                pmm(pb, K2, eyeS[:, :], (f"ivK2{sl}", "eyeS"), f"iv_k2t_{hc}")
                p.add("dve", lambda e, K2T=K2T, pb=pb: e.tensor_copy(
                    K2T, ps[pb][:, 0:64]),
                    r=(f"ps{pb}",), w=(f"ivK2T{sl}",), tag=f"iv_k2te_{hc}")
                pmm(pa, onesf[:, :], K2, (f"ivK2{sl}", "onesf"), f"iv_cs_{hc}")
                p.add("dve", lambda e, r2=r2, pa=pa: e.reduce_max(
                    r2, ps[pa][:, 0:64], axis=AX.X),
                    r=(f"ps{pa}",), w=(f"ivr2{sl}",), tag=f"iv_dm_{hc}")
                p.add("dve", lambda e, r2=r2, r3=r3: e.reciprocal(r3, r2),
                      r=(f"ivr2{sl}",), w=(f"ivr3{sl}",), tag=f"iv_dr_{hc}")
                p.add("dve", lambda e, V=V, K2T=K2T, r3=r3: e.tensor_scalar(
                    V, K2T, r3, None, ALU.mult),
                    r=(f"ivK2T{sl}", f"ivr3{sl}"), w=(f"ivV{sl}",), tag=f"iv_v0_{hc}")
                for it in range(6):
                    pmm(pb, V, eyeS[:, :], (f"ivV{sl}", "eyeS"), f"iv_vt_{hc}_{it}")
                    p.add("dve", lambda e, VT=VT, pb=pb: e.tensor_copy(
                        VT, ps[pb][:, 0:64]),
                        r=(f"ps{pb}",), w=(f"ivVT{sl}",), tag=f"iv_vte_{hc}_{it}")
                    pmm(pa, K2T, V, (f"ivK2T{sl}", f"ivV{sl}"), f"iv_w_{hc}_{it}")
                    p.add("dve", lambda e, Wm=Wm, pa=pa: e.tensor_copy(
                        Wm, ps[pa][:, 0:64]),
                        r=(f"ps{pa}",), w=(f"ivW{sl}",), tag=f"iv_we_{hc}_{it}")
                    pmm(pb, Wm, eyeS[:, :], (f"ivW{sl}", "eyeS"), f"iv_wt_{hc}_{it}")
                    p.add("dve", lambda e, WT=WT, pb=pb: e.tensor_copy(
                        WT, ps[pb][:, 0:64]),
                        r=(f"ps{pb}",), w=(f"ivWT{sl}",), tag=f"iv_wte_{hc}_{it}")
                    p.add("dve", lambda e, Pm=Pm, Wm=Wm: e.scalar_tensor_tensor(
                        Pm, eyeS[:, :], 7.0, Wm, ALU.mult, ALU.subtract),
                        r=(f"ivW{sl}", "eyeS", f"ivr0{sl}", f"ivr1{sl}", f"ivr2{sl}", f"ivr3{sl}"),
                        w=(f"ivP{sl}",), tag=f"iv_p1_{hc}_{it}")
                    pmm(pa, WT, Pm, (f"ivWT{sl}", f"ivP{sl}"), f"iv_p2m_{hc}_{it}")
                    p.add("dve", lambda e, Pm=Pm, pa=pa: e.scalar_tensor_tensor(
                        Pm, eyeS[:, :], 15.0, ps[pa][:, 0:64], ALU.mult,
                        ALU.subtract),
                        r=(f"ps{pa}", "eyeS", f"ivP{sl}"), w=(f"ivP{sl}",),
                        tag=f"iv_p2_{hc}_{it}")
                    pmm(pa, WT, Pm, (f"ivWT{sl}", f"ivP{sl}"), f"iv_p3m_{hc}_{it}")
                    p.add("dve", lambda e, Pm=Pm, pa=pa: e.scalar_tensor_tensor(
                        Pm, eyeS[:, :], 13.0, ps[pa][:, 0:64], ALU.mult,
                        ALU.subtract),
                        r=(f"ps{pa}", "eyeS", f"ivP{sl}"), w=(f"ivP{sl}",),
                        tag=f"iv_p3_{hc}_{it}")
                    pmm(pa, VT, Pm, (f"ivVT{sl}", f"ivP{sl}"), f"iv_vm_{hc}_{it}")
                    p.add("act", lambda e, V=V, pa=pa: e.activation(
                        V, ps[pa][:, 0:64], AF.Copy, scale=0.25),
                        r=(f"ps{pa}",), w=(f"ivV{sl}",), tag=f"iv_vn_{hc}_{it}")
                pmm(pb, V, eyeS[:, :], (f"ivV{sl}", "eyeS"), f"iv_ft_{hc}")

                def ff(eng, hc=hc, pb=pb):
                    return eng.tensor_copy(invT[:, hc, :], ps[pb][:, 0:64])
                p.add("dve", ff, r=(f"ps{pb}",), w=(f"invT{hc}",),
                      tag=f"iv_out_{hc}")

            for hc in range(HC):
                if hc == 7:
                    qT_block(7, extra=("xlmF",))
                e1_block(hc)
                inv_pair(hc)

            p.fence("F_attns", extra_w=("xlmT",))
            load_unit(2, qkv_unit(4), "ld_v0")
            load_unit(3, qkv_unit(5), "ld_v1")
            kT2 = attn_s[:, 0:1024].rearrange("q (hh t) -> q hh t", hh=4)
            vt = [attn_s[:, 1024 + 512 * i:1024 + 512 * (i + 1)]
                  for i in range(2)]

            def b1_half(half):
                ku = half
                vu = 2 + half
                for tq in range(8):
                    for hq in range(4):
                        bk = 2 if hq % 2 == 0 else 3
                        for kc in range(FC):
                            mm(bk, slots[ku][:, kc, 128 * hq:128 * (hq + 1)],
                               xnsv[:, kc, 256 * tq:256 * (tq + 1)],
                               kc == 0, kc == FC - 1,
                               (f"slot{ku}", f"xn{kc}_{tq // 2}"),
                               f"b1_km{half}_{tq}_{hq}_{kc}",
                               outsl=ps[bk][:, 0:256])

                        def fk(eng, hq=hq, bk=bk):
                            return eng.activation(kT2[:, hq, :],
                                                  ps[bk][:, 0:256], AF.Copy)
                        p.add("act", fk, r=(f"ps{bk}", "F_attns"),
                              w=(f"kT{hq}",), tag=f"b1_ke{half}_{tq}_{hq}")
                    for tt in range(2):
                        t = 2 * tq + tt
                        tk = tt
                        bkv = 0 if t % 2 == 0 else 1
                        for kc in range(FC):
                            mm(bkv, xnsv[:, kc, 128 * t:128 * (t + 1)],
                               slots[vu][:, kc, :], kc == 0, kc == FC - 1,
                               (f"slot{vu}", f"xn{kc}_{t // 4}"),
                               f"b1_vm{half}_{t}_{kc}")

                        def fv(eng, tk=tk, bkv=bkv):
                            return eng.activation(vt[tk][:, :], ps[bkv][:, :],
                                                  AF.Copy)
                        p.add("act", fv, r=(f"ps{bkv}", "F_attns"),
                              w=(f"vt{tk}",), tag=f"b1_ve{half}_{t}")
                        for hh in range(8):
                            h = 8 * half + hh
                            hc, par = h // 2, h % 2
                            hq = hc - 4 * half
                            bk = 4 if hh % 2 == 0 else 5
                            mm(bk, kT2[64 * par:64 * (par + 1), hq,
                                       128 * tt:128 * (tt + 1)],
                               qlmT[64 * par:64 * (par + 1), hc, :],
                               True, True, (f"kT{hq}", f"qlmT{hc}"),
                               f"b1_s3{half}_{t}_{hh}", outsl=ps[bk][:, 0:HD])

                            def f3(eng, hh=hh, bk=bk):
                                return eng.activation(e3t[:, hh, :],
                                                      ps[bk][:, 0:HD], AF.Exp)
                            p.add("act", f3, r=(f"ps{bk}",), w=(f"e3{hh}",),
                                  tag=f"b1_x3{half}_{t}_{hh}")
                            hpl = h // 2 - 4 * half
                            bsv = 6 + half
                            first = (t == 0 and hh == 0)
                            last = (t == NT - 1 and hh == 7)
                            mm(bsv, e3t[:, hh, :],
                               vt[tk][:, 64 * (2 * hq + par):
                                      64 * (2 * hq + par + 1)],
                               first, last, (f"e3g{g4}" if False else f"e3{hh}", f"vt{tk}"),
                               f"b1_sv{half}_{t}_{hh}",
                               outsl=ps[bsv][64 * par:64 * (par + 1),
                                             64 * hpl:64 * (hpl + 1)],
                               skipg=True)
                            mm(bsv, e3t[:, hh, :], onesb[:, 0:1],
                               False, last, (f"e3{hh}", "onesb"),
                               f"b1_z{half}_{t}_{hh}",
                               outsl=ps[bsv][64 * par:64 * (par + 1),
                                             256 + hpl:256 + hpl + 1],
                               skipg=True)
            pay_t = [cpay[:, 0:520], svz[:, 0:520]]
            sz_t = [cpay[:, 520:1040], svz[:, 520:1040]]
            pay_k = ["cpay0", "pay1"]
            sz_k = ["svz0", "svz1"]
            fence_k = ["F_sA", "F_sB"]

            def ship_half(half):
                bsv = 6 + half
                pay = pay_t[half]
                for hh in range(8):
                    h = 8 * half + hh
                    par, hpl = h % 2, h // 2 - 4 * half

                    def fs(eng, pay=pay, hh=hh, par=par, hpl=hpl, bsv=bsv):
                        return eng.tensor_copy(
                            pay[:, 64 * hh:64 * (hh + 1)],
                            ps[bsv][64 * par:64 * (par + 1),
                                    64 * hpl:64 * (hpl + 1)])
                    p.add("dve", fs, r=(f"ps{bsv}", fence_k[half]),
                          w=(pay_k[half],), tag=f"sv_e{half}_{hh}")

                    def fz(eng, pay=pay, hh=hh, par=par, hpl=hpl, bsv=bsv):
                        return eng.tensor_copy(
                            pay[:, 512 + hh:513 + hh],
                            ps[bsv][64 * par:64 * (par + 1),
                                    256 + hpl:256 + hpl + 1])
                    p.add("dve", fz, r=(f"ps{bsv}", fence_k[half]),
                          w=(pay_k[half],), tag=f"svz_e{half}_{hh}")
                p.dma("pool", cc2_in[half][:, :], pay, r=(pay_k[half],),
                      w=(f"cc2i{half}",), tag=f"cc2_put{half}")

                def fcc2(eng, half=half):
                    return eng.collective_compute(
                        "AllGather", ALU.bypass,
                        replica_groups=[[0, 1, 2, 3], [4, 5, 6, 7]],
                        ins=[cc2_in[half].ap().opt()],
                        outs=[cc2_out[half].ap().opt()])
                p.add("pool", fcc2, r=(f"cc2i{half}",), w=(f"cc2o{half}",),
                      kind="cc", tag=f"cc2_{half}")

            def combine_half(half):
                sz = sz_t[half]
                pay = pay_t[half]
                p.dma("pool", sz, cc2_out[half][0:64, :],
                      r=(f"cc2o{half}", fence_k[half]), w=(sz_k[half],),
                      tag=f"cc2_g0_{half}")
                for r_ in range(1, 4):
                    p.dma("pool", pay, cc2_out[half][64 * r_:64 * (r_ + 1), :],
                          r=(f"cc2o{half}", pay_k[half]), w=(pay_k[half],),
                          tag=f"cc2_g{r_}_{half}")

                    def fa(eng, sz=sz, pay=pay):
                        return eng.tensor_tensor(sz, sz, pay, ALU.add)
                    p.add("dve", fa, r=(sz_k[half], pay_k[half]),
                          w=(sz_k[half],), tag=f"cc2_a{r_}_{half}")

            def gsv_half(half):
                sz = sz_t[half]

                def frz(eng, half=half, sz=sz):
                    return eng.activation(rzh[:, 8 * half:8 * (half + 1)],
                                          sz[:, 512:520], AF.Ln)
                p.add("act", frz, r=(sz_k[half],), w=(f"rzh{half}",),
                      tag=f"rz_ln{half}")

                def frx(eng, half=half):
                    return eng.activation(rzh[:, 8 * half:8 * (half + 1)],
                                          rzh[:, 8 * half:8 * (half + 1)],
                                          AF.Exp, scale=-1.0)
                p.add("act", frx, r=(f"rzh{half}",), w=(f"rzh{half}",),
                      tag=f"rz_ex{half}")
                for hc in range(4 * half, 4 * half + 4):
                    for par in range(2):
                        h = 2 * hc + par
                        hh = h - 8 * half

                        def fn_(eng, h=h, sz=sz, hh=hh, par=par):
                            return eng.tensor_scalar(
                                sv3n[64 * par:64 * (par + 1), :],
                                sz[:, 64 * hh:64 * (hh + 1)],
                                rzh[:, h:h + 1], None, ALU.mult)
                        p.add("dve", fn_, r=(sz_k[half], f"rzh{half}"),
                              w=("sv3n",), tag=f"sv3n{h}")
                    bk = 4 if hc % 2 == 0 else 5
                    for par in range(2):
                        mm(bk, invT[64 * par:64 * (par + 1), hc, :],
                           sv3n[64 * par:64 * (par + 1), :], True, True,
                           (f"invT{hc}", "sv3n"), f"g_m{hc}_{par}",
                           outsl=ps[bk][64 * par:64 * (par + 1), 0:64],
                           tile=(64 * par, 64 * par))

                    def fg(eng, hc=hc, bk=bk):
                        return eng.tensor_copy(Gsb[:, hc, :],
                                               ps[bk][:, 0:64])
                    p.add("dve", fg, r=(f"ps{bk}",), w=(f"G{hc}",),
                          tag=f"g_e{hc}")
                for hc in range(4 * half, 4 * half + 4):
                    for pp in range(NP):
                        b1_ = 0 if pp % 2 == 0 else 1
                        b2_ = 2 if pp % 2 == 0 else 3
                        for par in range(2):
                            h = 2 * hc + par
                            mm(b1_, Gsb[64 * par:64 * (par + 1), hc, :],
                               E1[64 * par:64 * (par + 1), hc,
                                  512 * pp:512 * (pp + 1)],
                               True, True, (f"G{hc}", f"E1_{h}_{pp}"),
                               f"sv_m{h}_{pp}",
                               outsl=ps[b1_][64 * par:64 * (par + 1), :],
                               tile=(64 * par, 64 * par))
                            mm(b2_, onesb[64 * par:64 * (par + 1), 0:64],
                               E1[64 * par:64 * (par + 1), hc,
                                  512 * pp:512 * (pp + 1)],
                               True, True, (f"E1_{h}_{pp}", "onesb"),
                               f"svn_m{h}_{pp}",
                               outsl=ps[b2_][64 * par:64 * (par + 1), :],
                               tile=(64 * par, 64 * par))
                        pr = (4, 5) if (4 * hc + pp) % 2 == 0 else (2, 3)
                        lnb = poolf[pr[0]]
                        rzb = poolf[pr[1]]

                        def fl(eng, b2_=b2_, lnb=lnb):
                            return eng.activation(lnb[:, :], ps[b2_][:, :],
                                                  AF.Ln)
                        p.add("act", fl, r=(f"ps{b2_}",), w=(f"pf{pr[0]}",),
                              tag=f"sv_l{hc}_{pp}")

                        def fx(eng, lnb=lnb, rzb=rzb):
                            return eng.activation(rzb[:, :], lnb[:, :],
                                                  AF.Exp, scale=-1.0)
                        p.add("act", fx, r=(f"pf{pr[0]}",), w=(f"pf{pr[1]}",),
                              tag=f"sv_x{hc}_{pp}")

                        def fm(eng, hc=hc, pp=pp, b1_=b1_, rzb=rzb):
                            return eng.tensor_tensor(
                                xnsv[:, hc, 512 * pp:512 * (pp + 1)],
                                ps[b1_][:, :], rzb[:, :], ALU.mult)
                        p.add("dve", fm, r=(f"ps{b1_}", f"pf{pr[1]}"),
                              w=(f"xn{hc}_{pp}",), tag=f"sv_w{hc}_{pp}")

            b1_half(0)
            p.fence("F_sA", extra_w=("slot2",))
            ship_half(0)
            combine_half(0)
            b1_half(1)
            p.fence("F_sB", extra_w=("slot3",))
            ship_half(1)
            load_unit(0, proj_unit(0), "ld_pj0")
            load_unit(1, proj_unit(1), "ld_pj1")
            gsv_half(0)
            combine_half(1)
            gsv_half(1)
            for pp in range(NP):
                for m_ in range(FC):
                    bk = 4 + (pp * FC + m_) % 3
                    us = m_ // 4
                    for kc in range(FC):
                        mm(bk, slots[us][:, kc, 128 * (m_ % 4):128 * (m_ % 4 + 1)],
                           xnsv[:, kc, 512 * pp:512 * (pp + 1)],
                           kc == 0, kc == FC - 1,
                           (f"slot{us}", f"xn{kc}_{pp}"), f"pj_m{pp}_{m_}")

                    def fr_(eng, m_=m_, pp=pp, bk=bk):
                        return eng.tensor_tensor(
                            xF[:, m_, 512 * pp:512 * (pp + 1)],
                            xF[:, m_, 512 * pp:512 * (pp + 1)],
                            ps[bk][:, :], ALU.add)
                    p.add("dve", fr_, r=(f"ps{bk}", f"xF{m_}_{pp}"),
                          w=(f"xF{m_}_{pp}",), tag=f"pj_r{pp}_{m_}")

        def output():
            for hp_ in range(8):
                hsel = hp_ % 2
                pp = hp_ // 2
                for tt in range(2):
                    t = 2 * hp_ + tt
                    for fc in range(FC):
                        bk = (tt * FC + fc) % 4
                        transpose(bk, xF[:, fc, 128 * t:128 * (t + 1)],
                                  eyef[:, :], (f"xF{fc}_{pp}", "eyef"),
                                  f"tro{t}_{fc}", outsl=ps[bk][:, 0:128])

                        def fe(eng, hsel=hsel, tt=tt, fc=fc, bk=bk):
                            return eng.copy(
                                stg[:, 2 * hsel + tt, 128 * fc:128 * (fc + 1)],
                                ps[bk][:, 0:128])
                        p.add("act", fe, r=(f"ps{bk}",),
                              w=(f"ostg{hp_}", f"stgh{hsel}",
                                 f"slot{2 + hsel}", "cpay0", "pay1",
                                 "svz0", "svz1"),
                              tag=f"evo{t}_{fc}")
                p.dma("sp", y_h[256 * hp_:256 * (hp_ + 1), :].rearrange(
                    "(tp q) f -> q tp f", q=128),
                    stg[:, 2 * hsel:2 * hsel + 2, :],
                    r=(f"ostg{hp_}", f"stgh{hsel}", f"slot{2 + hsel}",
                       "cpay0", "pay1", "svz0", "svz1"),
                    w=(f"ost{hp_}",), tag=f"st_y{hp_}")

        has_attn = "attn" in phases

        def f1_post(tb):
            layer_norm("ln2", pieces=(2 * tb, 2 * tb + 1))
            lm_piece(2 * tb)
            lm_piece(2 * tb + 1)
            if tb == NTB - 1:
                pre_attn()
        if "ffn1" in phases:
            layer_norm("ln1")
            router("r1")
            tail = [(qkv_unit(0), "ld_q0"), (qkv_unit(1), "ld_q1"),
                    (qkv_unit(2), "ld_k0"),
                    (qkv_unit(3), "ld_k1")] if has_attn else ()
            ffn("f1", slot0=2, post_tb=f1_post if has_attn else None,
                tail_units=tail)
        if has_attn:
            if "ffn1" not in phases:
                layer_norm("ln2")
                for pp in range(NP):
                    lm_piece(pp)
                pre_attn()
            p.fence("m1fence",
                    extra_w=tuple(f"act{j}_{p5}" for j in range(NJ)
                                  for p5 in range(2)))
            attention()
        if "ffn2" in phases:
            layer_norm("ln3")
            router("r2")
            p.fence("m2fence",
                    extra_w=tuple(f"E1_{h}_{pp}" for h in range(H)
                                  for pp in range(NP)))
            p.fence("F2_slots", extra_w=("cpay0", "pay1", "svz0", "svz1"))
            ffn("f2", slot0=2, fences=("m2fence", "F2_slots"))
        output()

        p.finalize_and_emit(block, sems)
    return nc


_cache = {}


def _get_program(phases):
    key = tuple(phases)
    if key not in _cache:
        _cache[key] = build_program(phases)
    return _cache[key]


def _consts():
    eye = np.eye(128, dtype=np.float32)
    eyes = np.tile(np.eye(64, dtype=np.float32), (2, 1))
    esel = np.zeros((E, 1024), dtype=ml_dtypes.bfloat16)
    for e_ in range(E):
        esel[e_, 128 * e_:128 * (e_ + 1)] = 1.0
    return eye, eyes, esel


def kernel(L0_attn_x, L0_attn_w_qkv, L0_attn_w_proj, L0_ffn_W1, L0_ffn_b1,
           L0_ffn_W2, L0_ffn_b2, L0_ffn_Wr, L0_ffn_br):
    x = np.ascontiguousarray(np.asarray(L0_attn_x, dtype=np.float32))
    xf = x.reshape(2 * 8192, C)
    eye, eyes, esel_np = _consts()
    common = {
        "w_qkv": np.ascontiguousarray(np.asarray(L0_attn_w_qkv, np.float32)),
        "w_proj": np.ascontiguousarray(np.asarray(L0_attn_w_proj, np.float32)),
        "W1": np.ascontiguousarray(np.asarray(L0_ffn_W1, np.float32)),
        "b1": np.ascontiguousarray(np.asarray(L0_ffn_b1, np.float32)),
        "W2": np.ascontiguousarray(np.asarray(L0_ffn_W2, np.float32)),
        "b2": np.ascontiguousarray(np.asarray(L0_ffn_b2, np.float32)),
        "Wr": np.ascontiguousarray(np.asarray(L0_ffn_Wr, np.float32)),
        "br": np.ascontiguousarray(
            np.asarray(L0_ffn_br, np.float32)).reshape(1, E),
        "eye_f": eye,
        "eye64s": eyes,
        "esel": esel_np,
    }
    in_maps = []
    for c in range(NCORE):
        m = dict(common)
        m["x"] = np.ascontiguousarray(xf[T * c:T * (c + 1)])
        in_maps.append(m)
    nc = _get_program(PHASES)
    res = run_bass_kernel_spmd(nc, in_maps, core_ids=list(range(NCORE)))
    out = np.concatenate([np.asarray(r["y"], dtype=np.float32)
                          for r in res.results], axis=0)
    return out.reshape(2, 8192, C)
